# version 02
# baseline (speedup 1.0000x reference)
import numpy as np
from contextlib import ExitStack

import concourse.bass as bass
import concourse.mybir as mybir
from concourse.bass_utils import run_bass_kernel_spmd

F32 = mybir.dt.float32
BF16 = mybir.dt.bfloat16
AF = mybir.ActivationFunctionType
ALU = mybir.AluOpType
AX = mybir.AxisListType
PARANOID = True

T = 2048
D = 1024
NCH = D // 128
NH = 16
DH = 64
NQ = T // 512
NKT = T // 128
NCORE = 8

CSEM = {"vector": "c_vec", "scalar": "c_scal", "gpsimd": "c_pool",
        "tensor": "c_pe", "sync": "c_sync"}



def _rel_bucket(n, num_buckets=32, max_distance=128):
    import math
    n = np.asarray(n)
    max_exact = num_buckets // 2
    is_small = n < max_exact
    n_safe = np.maximum(n, 1).astype(np.float64)
    vil = max_exact + (np.log(n_safe / max_exact) / math.log(max_distance / max_exact)
                       * (num_buckets - max_exact)).astype(np.int32)
    vil = np.minimum(vil, num_buckets - 1)
    return np.where(is_small, n, vil)


def host_tables():
    t = {}
    t["IDENT"] = np.eye(128, dtype=np.float32)
    t["AIDENT"] = np.eye(128, dtype=np.float32)[::-1].copy()
    inv = 1.0 / (10000.0 ** (np.arange(0, DH, 2, dtype=np.float64) / DH))
    ang = inv[:, None] * np.arange(T, dtype=np.float64)[None, :]
    cos, sin = np.cos(ang), np.sin(ang)
    c64 = np.concatenate([cos, cos], axis=0)
    s64 = np.concatenate([sin, sin], axis=0)
    t["RCOS"] = np.tile(c64, (2, 1)).astype(np.float32)
    t["RSIN"] = np.tile(s64, (2, 1)).astype(np.float32)
    t["RCOSQ"] = (t["RCOS"] * 0.125).astype(np.float32)
    t["RSINQ"] = (t["RSIN"] * 0.125).astype(np.float32)
    p64 = np.zeros((64, 64), np.float32)
    for f in range(32):
        p64[32 + f, f] = -1.0
        p64[f, 32 + f] = 1.0
    t["RPERM"] = np.kron(np.eye(2, dtype=np.float32), p64)
    buckets = _rel_bucket(np.arange(T))
    XD = 4224
    oh = np.zeros((32, XD), np.float32)
    msk = np.zeros((1, XD), np.float32)
    for x in range(127, XD):
        d = x - 2174
        if d < 0:
            msk[0, x] = -1e30
        elif d < T:
            oh[buckets[d], x] = 0.125
    t["AOH"] = oh
    t["AMASK"] = msk
    invD = 1.0 / (10000.0 ** (np.arange(0, D, 2, dtype=np.float64) / D))
    angK = invD[:, None] * np.arange(T, dtype=np.float64)[None, :]
    t["RBASIS"] = np.concatenate([np.sin(angK), np.cos(angK)], 0).astype(np.float32)
    t["QCOS"] = np.cos(angK).astype(np.float32)
    t["QSIN"] = np.sin(angK).astype(np.float32)
    b32 = np.zeros((128, 4), np.float32)
    for j in range(4):
        b32[32 * j:32 * (j + 1), j] = 1.0
    t["BLK32"] = b32
    i32 = np.zeros((128, 32), np.float32)
    for p in range(128):
        i32[p, p % 32] = 1.0
    t["I32P"] = i32
    t["I32PT"] = np.ascontiguousarray(i32.T)
    b64 = np.zeros((128, 2), np.float32)
    b64[0:64, 0] = 1.0
    b64[64:128, 1] = 1.0
    t["BLK64"] = b64
    return t


def permute_head_evens_odds(W, b=None):
    perm = np.zeros(D, np.int64)
    for h in range(NH):
        base = h * DH
        perm[base:base + 32] = base + np.arange(0, DH, 2)
        perm[base + 32:base + DH] = base + np.arange(1, DH, 2)
    Wp = np.ascontiguousarray(W[:, perm])
    if b is None:
        return Wp
    return Wp, np.ascontiguousarray(b[perm])


def permute_cols_evens_odds(W):
    perm = np.concatenate([np.arange(0, D, 2), np.arange(1, D, 2)])
    return np.ascontiguousarray(W[:, perm])



class Sem:
    __slots__ = ("h", "n")

    def __init__(self, h):
        self.h = h
        self.n = 0


class Ctx:
    def __init__(self, nc, stack):
        self.nc = nc
        self.stack = stack
        self.eng_ops = {e: [] for e in CSEM}
        self.sems = {}
        self.bar_k = 0
        self.consts = {}
        self.CONST = None
        self._uid = 0
        self.rr = {}

    def uid(self):
        self._uid += 1
        return self._uid

    def sem(self, name):
        if name not in self.sems:
            h = self.stack.enter_context(self.nc.semaphore(name))
            self.sems[name] = Sem(h)
        return self.sems[name]

    def _append(self, eng, fn):
        self.eng_ops[eng].append(fn)

    def op(self, eng, fn, inc=True):
        s = self.sem(CSEM[eng])
        prev = s.n
        if inc:
            s.n += 1

            def g(e, fn=fn, prev=prev):
                if prev:
                    e.wait_ge(s.h, prev)
                ins = fn(e)
                ins.then_inc(s.h, 1)
        else:
            def g(e, fn=fn, prev=prev):
                if prev:
                    e.wait_ge(s.h, prev)
                fn(e)
        self._append(eng, g)
        return s.n

    def pe(self, fn, inc=False):
        s = self.sem("c_pe")
        if inc:
            s.n += 1

            def g(e, fn=fn):
                fn(e).then_inc(s.h, 1)
        else:
            g = fn
        self._append("tensor", g)
        return s.n

    DMA_POOL = 6

    def dma(self, eng, out, in_, after=(), **kw):
        idx = self.rr.get(eng, 0)
        self.rr[eng] = (idx + 1) % self.DMA_POOL
        name = f"d_{eng}{idx}"
        s = self.sem(name)
        prev = s.n
        s.n += 16
        tgt = s.n

        def g(e, out=out, in_=in_, kw=kw, after=tuple(after), prev=prev):
            if prev:
                e.wait_ge(s.h, prev)
            for nm, cnt in after:
                e.wait_ge(self.sems[nm].h, cnt)
            e.dma_start(out=out, in_=in_, **kw).then_inc(s.h, 16)
        self._append(eng, g)
        return (name, tgt)

    def wait_dma(self, eng, token):
        name, tgt = token
        self.wait(eng, name, tgt)

    def wait_dma_all(self, eng, queue):
        for idx in range(self.DMA_POOL):
            name = f"d_{queue}{idx}"
            if name in self.sems:
                self.wait(eng, name, self.sems[name].n)

    def wait(self, eng, sem, tgt=None):
        s = self.sem(sem)
        t = s.n if tgt is None else tgt
        if t <= 0:
            return

        def g(e, t=t):
            e.wait_ge(s.h, t)
        self._append(eng, g)

    def barrier(self):
        self.bar_k += 1
        k = self.bar_k
        bar = self.sem("bar")
        bar.n = 5 * k
        for eng in CSEM:
            s = self.sem(CSEM[eng])
            n = s.n

            def g(e, k=k, bar=bar, s=s, n=n):
                if n:
                    e.wait_ge(s.h, n)
                e.nop().then_inc(bar.h, 1)
                e.wait_ge(bar.h, 5 * k)
            self._append(eng, g)

    def const(self, val):
        val = float(val)
        if val not in self.consts:
            self.consts[val] = len(self.consts)
            assert len(self.consts) <= 24
        return self.CONST[:, self.consts[val]:self.consts[val] + 1]

    def finalize(self, block):
        for eng in CSEM:
            ops = self.eng_ops[eng]
            getattr(block, eng)(lambda e, ops=ops: [f(e) for f in ops])

    def sb(self, stack, name, shape, dtype):
        return stack.enter_context(
            self.nc.sbuf_tensor(f"{name}{self.uid()}", shape, dtype))

    def ps(self, stack, name, shape, dtype=F32):
        return stack.enter_context(
            self.nc.psum_tensor(f"{name}{self.uid()}", shape, dtype))



def new_program(tables_needed=("IDENT",)):
    nc = bass.Bass("TRN2", target_bir_lowering=False)
    stack = ExitStack()
    ctx = Ctx(nc, stack)
    g = {}
    g["_tables_needed"] = set(tables_needed)
    return nc, stack, ctx, g


def ext_in(nc, name, shape, dtype=F32):
    return nc.dram_tensor(name, list(shape), dtype, kind="ExternalInput").ap()


def ext_out(nc, name, shape, dtype=F32):
    return nc.dram_tensor(name, list(shape), dtype, kind="ExternalOutput").ap()


def alloc_common(ctx, stack, g, nc, lnx=True, big_ones=False):
    g["_big_ones"] = big_ones
    if lnx:
        g["LNX"] = ctx.sb(stack, "LNX", [128, NCH * T], BF16)
    g["CONST"] = ctx.sb(stack, "CONST", [128, 24], F32)
    ctx.CONST = g["CONST"]
    g["ONESB"] = ctx.sb(stack, "ONESB", [128, 1], BF16)
    g["ONESF"] = ctx.sb(stack, "ONESF", [128, 1], F32)
    g["ONESROWF"] = ctx.sb(stack, "ONESROWF", [1, 128], F32)
    g["OSCROW"] = ctx.sb(stack, "OSCROW", [1, 128], F32)
    if big_ones:
        g["ONES512F"] = ctx.sb(stack, "ONES512F", [1, 512], F32)
        g["ONES512B"] = ctx.sb(stack, "ONES512B", [1, 512], BF16)
    g["ONESROWB"] = ctx.sb(stack, "ONESROWB", [1, 128], BF16)


def prelude(ctx, g, table_loads):
    def fv(e):
        ins = None
        for val, idx in ctx.consts.items():
            ins = e.memset(ctx.CONST[:, idx:idx + 1], float(val))
        ins = e.memset(g["ONESB"][:, :], 1.0)
        ins = e.memset(g["ONESF"][:, :], 1.0)
        ins = e.memset(g["ONESROWF"][:, :], 1.0)
        ins = e.memset(g["OSCROW"][:, :], 1.0 / D)
        if g.get("_big_ones"):
            ins = e.memset(g["ONES512F"][:, :], 1.0)
            ins = e.memset(g["ONES512B"][:, :], 1.0)
        ins = e.memset(g["ONESROWB"][:, :], 1.0)
        return ins
    ctx.op("vector", fv)
    for fn in table_loads:
        fn()
    ctx.barrier()



def emit_ln(ctx, g, xT, ln_eps=1e-5, tap_rows_to=None):
    with ExitStack() as st:
        psA = ctx.ps(st, "lnA", [128, T])
        psB = ctx.ps(st, "lnB", [128, T])
        SQ = ctx.sb(st, "SQ", [128, 2 * T], BF16)
        STG = ctx.sb(st, "STG", [128, 2 * T], F32)
        SROWS = ctx.sb(st, "SROWS", [1, 3 * T], F32)
        LNX = g["LNX"]
        cast_cnt = {}
        allmean = 0
        for c in range(NCH):
            buf = STG[:, (c % 2) * T:(c % 2) * T + T]
            aft = [("c_vec", cast_cnt[c - 2])] if c >= 2 else []
            tok = ctx.dma("sync", buf, xT[c * 128:(c + 1) * 128, 0:T], after=aft)
            ctx.wait_dma("vector", tok)
            cast_cnt[c] = ctx.op("vector", lambda e, b=buf, c=c: e.tensor_copy(
                out=LNX[:, c * T:c * T + T], in_=b))
            ctx.wait("tensor", "c_vec", cast_cnt[c])
            for k in range(4):
                allmean = ctx.pe(lambda e, c=c, k=k: e.matmul(
                    psA[0:1, 512 * k:512 * k + 512], g["ONESB"][:, 0:1],
                    LNX[:, c * T + 512 * k:c * T + 512 * k + 512],
                    start=(c == 0), stop=(c == NCH - 1)),
                    inc=(c == NCH - 1 and k == 3))
        if PARANOID:
            ctx.barrier()
        ctx.wait("scalar", "c_pe", allmean)
        mucp = ctx.op("scalar", lambda e: e.copy(SROWS[0:1, 0:T], psA[0:1, 0:T]))
        if PARANOID:
            ctx.barrier()
        ctx.wait("tensor", "c_scal", mucp)
        bcmu = 0
        for k in range(4):
            bcmu = ctx.pe(lambda e, k=k: e.matmul(
                psB[:, 512 * k:512 * k + 512], g["OSCROW"][0:1, :],
                SROWS[0:1, 512 * k:512 * k + 512], start=True, stop=True),
                inc=(k == 3))
        if PARANOID:
            ctx.barrier()
        ctx.wait("vector", "c_pe", bcmu)
        sqmm = {}
        allsq = 0
        for c in range(NCH):
            cen = ctx.op("vector", lambda e, c=c: e.tensor_tensor(
                LNX[:, c * T:c * T + T], LNX[:, c * T:c * T + T],
                psB[:, 0:T], ALU.subtract))
            sqb = SQ[:, (c % 2) * T:(c % 2) * T + T]
            if c >= 2:
                ctx.wait("scalar", "c_pe", sqmm[c - 2])
            ctx.wait("scalar", "c_vec", cen)
            sc = ctx.op("scalar", lambda e, s=sqb, c=c: e.activation(
                out=s, in_=LNX[:, c * T:c * T + T], func=AF.Square))
            ctx.wait("tensor", "c_scal", sc)
            for k in range(4):
                sqmm[c] = ctx.pe(lambda e, s=sqb, k=k, c=c: e.matmul(
                    psA[0:1, 512 * k:512 * k + 512], g["ONESB"][:, 0:1],
                    s[:, 512 * k:512 * k + 512],
                    start=(c == 0), stop=(c == NCH - 1)), inc=(k == 3))
            allsq = sqmm[c]
        if PARANOID:
            ctx.barrier()
        ctx.wait("scalar", "c_pe", allsq)
        invD_ap = ctx.const(1.0 / D)[0:1, 0:1]
        eps_ap = ctx.const(ln_eps)[0:1, 0:1]
        mhalf_ap = ctx.const(-0.5)[0:1, 0:1]
        ctx.op("scalar", lambda e: e.activation(
            out=SROWS[0:1, 2 * T:3 * T], in_=psA[0:1, 0:T], func=AF.Ln,
            scale=invD_ap, bias=eps_ap))
        rcnt = ctx.op("scalar", lambda e: e.activation(
            out=SROWS[0:1, T:2 * T], in_=SROWS[0:1, 2 * T:3 * T], func=AF.Exp,
            scale=mhalf_ap))
        if PARANOID:
            ctx.barrier()
        ctx.wait("tensor", "c_scal", rcnt)
        bcr = 0
        for k in range(4):
            bcr = ctx.pe(lambda e, k=k: e.matmul(
                psB[:, 512 * k:512 * k + 512], g["ONESROWF"][0:1, :],
                SROWS[0:1, T + 512 * k:T + 512 * k + 512], start=True, stop=True),
                inc=(k == 3))
        if PARANOID:
            ctx.barrier()
        ctx.wait("vector", "c_pe", bcr)
        for c in range(NCH):
            ctx.op("vector", lambda e, c=c: e.tensor_tensor(
                LNX[:, c * T:c * T + T], LNX[:, c * T:c * T + T],
                psB[:, 0:T], ALU.mult))
        if tap_rows_to is not None:
            vnow = ctx.sem("c_vec").n
            snow = ctx.sem("c_scal").n
            for r in range(2):
                ctx.dma("gpsimd", tap_rows_to[r:r + 1, 0:T], SROWS[0:1, r * T:r * T + T],
                        after=[("c_vec", vnow), ("c_scal", snow)])
            ctx.wait_dma_all("gpsimd", "gpsimd")
        ctx.barrier()



def stage_w(ctx, st, name, w_hbm, rows=D, cols=D, dtype=BF16):
    nchr = rows // 128
    wb = ctx.sb(st, name, [128, nchr * cols], dtype)
    src = bass.AP(w_hbm.tensor, w_hbm.offset, [[cols, 128], [128 * cols, nchr], [1, cols]])
    dst = wb[:].rearrange("p (c m) -> p c m", c=nchr)
    tgt = ctx.dma("gpsimd", dst, src)
    return wb, tgt


def load_cols(ctx, st, name, v_hbm, n, dtype=F32):
    cols = n // 128
    vb = ctx.sb(st, name, [128, cols], dtype)
    src = bass.AP(v_hbm.tensor, v_hbm.offset, [[1, 128], [128, cols]])
    tgt = ctx.dma("gpsimd", vb[:], src, allow_slow_non_contiguous=True)
    return vb, tgt


def load_row(ctx, st, name, v_hbm, n, dtype=F32):
    vb = ctx.sb(st, name, [1, n], dtype)
    src = bass.AP(v_hbm.tensor, v_hbm.offset, [[n, 1], [1, n]])
    tgt = ctx.dma("gpsimd", vb[:], src)
    return vb, tgt


def load_tab(ctx, st, name, hbm, shape, dtype=BF16):
    tb = ctx.sb(st, name, list(shape), dtype)
    tgt = ctx.dma("gpsimd", tb[0:shape[0], 0:shape[1]], hbm[0:shape[0], 0:shape[1]])
    return tb, tgt



def emit_qk_rope(ctx, g, st_phase, wb, wtgt, bcol, btgt, out_sb, cosT, sinT, permT,
                 rhs_sb=None, rhs_off=None):
    LNX = rhs_sb if rhs_sb is not None else g["LNX"]
    with ExitStack() as st:
        psP = ctx.ps(st, "qkP", [128, T])
        psR = ctx.ps(st, "qkR", [128, T])
        T0 = ctx.sb(st, "T0", [128, T], F32)
        T1 = ctx.sb(st, "T1", [128, T], BF16)
        ctx.wait_dma("tensor", wtgt)
        ctx.wait_dma("vector", btgt)
        last_out = 0
        for m in range(NCH):
            if m >= 1:
                ctx.wait("tensor", "c_vec", last_out)
            pm = 0
            for k in range(4):
                for c in range(NCH):
                    pm = ctx.pe(lambda e, m=m, k=k, c=c: e.matmul(
                        psP[:, 512 * k:512 * k + 512],
                        wb[:, c * D + m * 128:c * D + m * 128 + 128],
                        LNX[:, c * T + 512 * k:c * T + 512 * k + 512],
                        start=(c == 0), stop=(c == NCH - 1)),
                        inc=(c == NCH - 1 and k == 3))
            ctx.wait("vector", "c_pe", pm)
            t0c = ctx.op("vector", lambda e, m=m: e.tensor_scalar_add(
                T0[:], psP[:, 0:T], bcol[:, m:m + 1]))
            ctx.wait("tensor", "c_vec", t0c)
            pr = 0
            for k in range(4):
                pr = ctx.pe(lambda e, k=k: e.matmul(
                    psR[:, 512 * k:512 * k + 512], permT[:, :],
                    T0[:, 512 * k:512 * k + 512], start=True, stop=True),
                    inc=(k == 3))
            ctx.op("vector", lambda e: e.tensor_tensor(T1[:], T0[:], cosT[:, 0:T], ALU.mult))
            ctx.wait("vector", "c_pe", pr)
            ctx.op("vector", lambda e: e.scalar_tensor_tensor(
                out=T0[:], in0=psR[:, 0:T], scalar=1.0, in1=sinT[:, 0:T],
                op0=ALU.mult, op1=ALU.mult))
            last_out = ctx.op("vector", lambda e, m=m: e.tensor_tensor(
                out_sb[:, m * T:m * T + T], T1[:], T0[:], ALU.add))
        ctx.barrier()


def emit_v_tok(ctx, g, st_phase, wb, wtgt, brow, btgt, VA, n_aug=65):
    LNX = g["LNX"]
    with ExitStack() as st:
        psV = ctx.ps(st, "vP", [128, 2 * 512])
        psBV = ctx.ps(st, "vB", [128, D])
        BVB = ctx.sb(st, "BVB", [128, D], F32)
        ones_view = VA[:].rearrange("p (t h c) -> p t h c", t=NKT, h=NH, c=n_aug)[:, :, :, 64:65]
        ctx.op("vector", lambda e: e.memset(ones_view, 1.0))
        ctx.wait_dma("tensor", btgt)
        pb = 0
        for k in range(2):
            pb = ctx.pe(lambda e, k=k: e.matmul(
                psBV[:, 512 * k:512 * k + 512], g["ONESROWF"][0:1, :],
                brow[0:1, 512 * k:512 * k + 512], start=True, stop=True), inc=(k == 1))
        ctx.wait("vector", "c_pe", pb)
        ctx.op("vector", lambda e: e.tensor_copy(BVB[:], psBV[:, 0:D]))
        ctx.wait_dma("tensor", wtgt)
        evac_cnt = {}
        for tt in range(NKT):
            for k in range(2):
                it = tt * 2 + k
                sl = it % 2
                if it >= 2:
                    ctx.wait("tensor", "c_vec", evac_cnt[it - 2])
                pm = 0
                for c in range(NCH):
                    pm = ctx.pe(lambda e, tt=tt, k=k, c=c, sl=sl: e.matmul(
                        psV[:, sl * 512:sl * 512 + 512],
                        LNX[:, c * T + tt * 128:c * T + tt * 128 + 128],
                        wb[:, c * D + k * 512:c * D + k * 512 + 512],
                        start=(c == 0), stop=(c == NCH - 1)),
                        inc=(c == NCH - 1))
                dst = VA[:].rearrange("p (t h c) -> p t h c", t=NKT, h=NH, c=n_aug)[
                    :, tt:tt + 1, 8 * k:8 * k + 8, 0:64]
                ctx.wait("vector", "c_pe", pm)
                evac_cnt[it] = ctx.op("vector", lambda e, dst=dst, sl=sl, k=k: e.tensor_tensor(
                    dst, psV[:, sl * 512:sl * 512 + 512].rearrange(
                        "p (o h c) -> p o h c", o=1, h=8),
                    BVB[:, k * 512:k * 512 + 512].rearrange(
                        "p (o h c) -> p o h c", o=1, h=8), ALU.add))
        ctx.barrier()



def emit_attn_heads(ctx, g, QF, KF, VA, ATTO, expC, extra_score_mm=None,
                    causal=False, skew=None):
    with ExitStack() as st:
        psS = ctx.ps(st, "psS", [128, 2 * 512])
        psO = ctx.ps(st, "psO", [128, 512])
        psBC = ctx.ps(st, "psBC", [128, 512])
        ATT = ctx.sb(st, "ATT", [128, NKT * 512], BF16)
        SINV = ctx.sb(st, "SINV", [128, 512], F32)
        SSROW = ctx.sb(st, "SSROW", [1, 512], F32)
        expC_ap = ctx.const(-float(expC))
        SKJ = None
        if skew is not None:
            SKJ = ctx.sb(st, "SKJ", [128, 2 * 4096], BF16)
            sk_tok = {}
            sk_pe_guard = {0: 0, 1: 0}

            def load_skew(h):
                sl = h % 2
                aft = [("c_pe", sk_pe_guard[sl])] if sk_pe_guard[sl] else []
                src_ap = bass.AP(skew["btab_hbm"].tensor, h * 4224, [[1, 128], [1, 4096]])
                sk_tok[h] = ctx.dma("sync", SKJ[:, sl * 4096:sl * 4096 + 4096],
                                    src_ap, after=aft)
            load_skew(0)
        for h in range(NH):
            ch = h // 2
            base = 64 * (h % 2)
            if SKJ is not None:
                ctx.wait_dma("tensor", sk_tok[h])
            for qb in range(NQ):
                ctx.wait("tensor", "c_scal", ctx.sem("c_scal").n)
                exp_cnt = {}
                av_cnt = {}

                nkt_v = min(NKT, 4 * qb + 4) if causal else NKT

                def emit_av(kt):
                    ctx.wait("tensor", "c_scal", exp_cnt[kt])
                    av_cnt[kt] = ctx.pe(lambda e, h=h, kt=kt, last=(kt == nkt_v - 1): e.matmul(
                        psO[0:65, 0:512],
                        VA[:, kt * NH * 65 + h * 65:kt * NH * 65 + h * 65 + 65],
                        ATT[:, kt * 512:kt * 512 + 512],
                        start=(kt == 0), stop=last),
                        inc=(kt == nkt_v - 1))

                for kt in range(nkt_v):
                    sl = kt % 2
                    if kt >= 2:
                        ctx.wait("tensor", "c_scal", exp_cnt[kt - 2])
                    has_extra = (extra_score_mm is not None) or (SKJ is not None)
                    pm = ctx.pe(lambda e, ch=ch, base=base, qb=qb, kt=kt, sl=sl,
                                he=has_extra: e.matmul(
                        psS[:, sl * 512:sl * 512 + 512],
                        KF[base:base + 64, ch * T + kt * 128:ch * T + kt * 128 + 128],
                        QF[base:base + 64, ch * T + qb * 512:ch * T + qb * 512 + 512],
                        start=True, stop=(not he)),
                        inc=(not has_extra))
                    if SKJ is not None:
                        off = (h % 2) * 4096 + (qb * 512 - kt * 128) + 2047
                        pm = ctx.pe(lambda e, off=off, sl=sl: e.matmul(
                            psS[:, sl * 512:sl * 512 + 512], skew["AID"][:, :],
                            SKJ[:, off:off + 512], start=False, stop=True), inc=True)
                    elif extra_score_mm is not None:
                        pm = extra_score_mm(h, qb, kt, psS, sl)
                    ctx.wait("scalar", "c_pe", pm)
                    exp_cnt[kt] = ctx.op("scalar", lambda e, kt=kt, sl=sl, b=expC_ap: e.activation(
                        out=ATT[:, kt * 512:kt * 512 + 512],
                        in_=psS[:, sl * 512:sl * 512 + 512], func=AF.Exp,
                        bias=b))
                    if kt >= 1:
                        emit_av(kt - 1)
                emit_av(nkt_v - 1)
                if SKJ is not None and qb == 0 and h + 1 < NH:
                    sk_pe_guard[(h + 1) % 2] = ctx.sem("c_pe").n
                    load_skew(h + 1)
                ctx.wait("scalar", "c_pe", av_cnt[nkt_v - 1])
                sscp = ctx.op("scalar", lambda e: e.copy(SSROW[0:1, :], psO[64:65, 0:512]))
                ctx.wait("tensor", "c_scal", sscp)
                pb = ctx.pe(lambda e: e.matmul(
                    psBC[0:64, 0:512], g["ONESROWF"][0:1, 0:64], SSROW[0:1, :],
                    start=True, stop=True), inc=True)
                ctx.wait("vector", "c_pe", pb)
                cp = ctx.op("vector", lambda e: e.reciprocal(SINV[0:64, :], psBC[0:64, 0:512]))
                ec = ctx.op("vector", lambda e, ch=ch, base=base, qb=qb: e.tensor_tensor(
                    ATTO[base:base + 64, ch * T + qb * 512:ch * T + qb * 512 + 512],
                    psO[0:64, 0:512], SINV[0:64, :], ALU.mult))
                ctx.wait("tensor", "c_vec", ec)
        ctx.barrier()



def emit_proj_resadd(ctx, g, wb, wtgt, bcol, btgt, rhs_sb, xT, yT,
                     rhs_is_lnx=False):
    with ExitStack() as st:
        ps = [ctx.ps(st, "prA", [128, T]), ctx.ps(st, "prB", [128, T])]
        STG = ctx.sb(st, "STG", [128, 2 * T], F32)
        OSTG = ctx.sb(st, "OSTG", [128, 2 * T], F32)
        ctx.wait_dma("tensor", wtgt)
        ctx.wait_dma("vector", btgt)
        evac_cnt = {}
        xin_tgt = {}
        out_tgt = {}
        for m in range(NCH):
            pcur = ps[m % 2]
            aft = [("c_vec", evac_cnt[m - 2])] if m >= 2 else []
            xin_tgt[m] = ctx.dma("sync", STG[:, (m % 2) * T:(m % 2) * T + T],
                                 xT[m * 128:(m + 1) * 128, 0:T], after=aft)
            if m >= 2:
                ctx.wait("tensor", "c_vec", evac_cnt[m - 2])
            pm = 0
            for k in range(4):
                for c in range(NCH):
                    pm = ctx.pe(lambda e, m=m, k=k, c=c, p=pcur: e.matmul(
                        p[:, 512 * k:512 * k + 512],
                        wb[:, c * D + m * 128:c * D + m * 128 + 128],
                        rhs_sb[:, c * T + 512 * k:c * T + 512 * k + 512],
                        start=(c == 0), stop=(c == NCH - 1)),
                        inc=(c == NCH - 1 and k == 3))
            ctx.wait("vector", "c_pe", pm)
            ctx.wait_dma("vector", xin_tgt[m])
            if m >= 2:
                ctx.wait_dma("vector", out_tgt[m - 2])
            evac_cnt[m] = ctx.op("vector", lambda e, m=m, p=pcur: e.scalar_tensor_tensor(
                out=OSTG[:, (m % 2) * T:(m % 2) * T + T],
                in0=p[:, 0:T], scalar=bcol[:, m:m + 1],
                in1=STG[:, (m % 2) * T:(m % 2) * T + T],
                op0=ALU.add, op1=ALU.add))
            out_tgt[m] = ctx.dma("sync", yT[m * 128:(m + 1) * 128, 0:T],
                                 OSTG[:, (m % 2) * T:(m % 2) * T + T],
                                 after=[("c_vec", evac_cnt[m])])
        ctx.wait_dma_all("sync", "sync")
        ctx.barrier()



def build_l1_attn_b(upto="all"):
    nc, stack, ctx, g = new_program()
    xT = ext_in(nc, "xT", [D, T])
    yT = ext_out(nc, "yT", [D, T])
    Wq = ext_in(nc, "Wq", [D, D]); bq = ext_in(nc, "bq", [D])
    Wk = ext_in(nc, "Wk", [D, D]); bk = ext_in(nc, "bk", [D])
    Wv = ext_in(nc, "Wv", [D, D]); bv = ext_in(nc, "bv", [D])
    Wp = ext_in(nc, "Wp", [D, D]); bp = ext_in(nc, "bp", [D])
    hCOSQ = ext_in(nc, "hCOSQ", [128, T]); hSINQ = ext_in(nc, "hSINQ", [128, T])
    hCOS = ext_in(nc, "hCOS", [128, T]); hSIN = ext_in(nc, "hSIN", [128, T])
    hPERM = ext_in(nc, "hPERM", [128, 128])

    def tap_out(sb):
        for c in range(NCH):
            ctx.dma("gpsimd", yT[c * 128:(c + 1) * 128, 0:T], sb[:, c * T:c * T + T])
        ctx.wait_dma_all("gpsimd", "gpsimd")
        ctx.barrier()

    with stack:
        alloc_common(ctx, stack, g, nc)
        ATTO = ctx.sb(stack, "ATTO", [128, NCH * T], BF16)
        prelude(ctx, g, [])
        emit_ln(ctx, g, xT, tap_rows_to=(yT if upto == "lnstats" else None))
        if upto == "lnstats":
            with nc.Block() as block:
                ctx.finalize(block)
            return nc
        if upto == "ln":
            tap_out(g["LNX"])
            with nc.Block() as block:
                ctx.finalize(block)
            return nc
        with ExitStack() as sta:
            QF = ctx.sb(sta, "QF", [128, NCH * T], BF16)
            KF = ctx.sb(sta, "KF", [128, NCH * T], BF16)
            VA = ctx.sb(sta, "VA", [128, NKT * NH * 65], BF16)
            COSQ, t1 = load_tab(ctx, sta, "COSQ", hCOSQ, (128, T))
            SINQ, t2 = load_tab(ctx, sta, "SINQ", hSINQ, (128, T))
            COS, t3 = load_tab(ctx, sta, "COS", hCOS, (128, T))
            SIN, t4 = load_tab(ctx, sta, "SIN", hSIN, (128, T))
            PERM, t5 = load_tab(ctx, sta, "PERM", hPERM, (128, 128), dtype=F32)
            for tk in (t1, t2, t3, t4, t5):
                ctx.wait_dma("vector", tk)
                ctx.wait_dma("tensor", tk)
            with ExitStack() as stq:
                wbq, wt = stage_w(ctx, stq, "Wq", Wq)
                bqc, bt = load_cols(ctx, stq, "bq", bq, D)
                emit_qk_rope(ctx, g, stq, wbq, wt, bqc, bt, QF, COSQ, SINQ, PERM)
            with ExitStack() as stk:
                wbk, wt = stage_w(ctx, stk, "Wk", Wk)
                bkc, bt = load_cols(ctx, stk, "bk", bk, D)
                emit_qk_rope(ctx, g, stk, wbk, wt, bkc, bt, KF, COS, SIN, PERM)
            with ExitStack() as stv:
                wbv, wt = stage_w(ctx, stv, "Wv", Wv)
                bvr, bt = load_row(ctx, stv, "bv", bv, D)
                emit_v_tok(ctx, g, stv, wbv, wt, bvr, bt, VA)
            if upto == "qkv":
                tap_out(QF)
                with nc.Block() as block:
                    ctx.finalize(block)
                return nc
            emit_attn_heads(ctx, g, QF, KF, VA, ATTO, expC=20.0)
            if upto == "heads":
                tap_out(ATTO)
                with nc.Block() as block:
                    ctx.finalize(block)
                return nc
        with ExitStack() as stp:
            wbp, wt = stage_w(ctx, stp, "Wp", Wp)
            bpc, bt = load_cols(ctx, stp, "bp", bp, D)
            emit_proj_resadd(ctx, g, wbp, wt, bpc, bt, ATTO, xT, yT)
        with nc.Block() as block:
            ctx.finalize(block)
    return nc



def emit_btab_prep(ctx, g, st, rel_tab_hbm, hOH, hMASK, btab_hbm):
    XD = 4224
    with ExitStack() as stp:
        TA = ctx.sb(stp, "TA", [32, 32], BF16)
        OHS = ctx.sb(stp, "OHS", [32, XD], BF16)
        MR = ctx.sb(stp, "MR", [1, XD], BF16)
        BTB = ctx.sb(stp, "BTB", [32, XD], BF16)
        psB = ctx.ps(stp, "btp", [32, 512])
        zc = ctx.op("vector", lambda e: e.memset(TA[:, :], 0.0))
        t1 = ctx.dma("gpsimd", TA[0:32, 0:16], rel_tab_hbm[0:32, 0:16],
                     after=[("c_vec", zc)])
        t2 = ctx.dma("gpsimd", OHS[:], hOH[0:32, 0:XD])
        t3 = ctx.dma("gpsimd", MR[:], hMASK[0:1, 0:XD])
        for tk in (t1, t2, t3):
            ctx.wait_dma("tensor", tk)
        ev = {}
        x0s = list(range(0, XD, 512))
        for i, x0 in enumerate(x0s):
            N = min(512, XD - x0)
            if i >= 1:
                ctx.wait("tensor", "c_vec", ev[i - 1])
            ctx.pe(lambda e, x0=x0, N=N: e.matmul(
                psB[0:32, 0:N], g["ONESROWB"][0:1, 0:32], MR[0:1, x0:x0 + N],
                start=True, stop=False))
            pm = ctx.pe(lambda e, x0=x0, N=N: e.matmul(
                psB[0:32, 0:N], TA[:, 0:32], OHS[:, x0:x0 + N],
                start=False, stop=True), inc=True)
            ctx.wait("vector", "c_pe", pm)
            ev[i] = ctx.op("vector", lambda e, x0=x0, N=N: e.tensor_copy(
                BTB[0:32, x0:x0 + N], psB[0:32, 0:N]))
        vnow = ctx.sem("c_vec").n
        tok = ctx.dma("sync", btab_hbm[0:16, 0:XD], BTB[0:16, 0:XD],
                      after=[("c_vec", vnow)])
        ctx.wait_dma("sync", tok)
        ctx.barrier()


def build_l0_attn_a(upto="all"):
    nc, stack, ctx, g = new_program()
    xT = ext_in(nc, "xT", [D, T])
    yT = ext_out(nc, "yT", [D, T])
    Wq = ext_in(nc, "Wq", [D, D]); bq = ext_in(nc, "bq", [D])
    Wk = ext_in(nc, "Wk", [D, D]); bk = ext_in(nc, "bk", [D])
    Wv = ext_in(nc, "Wv", [D, D]); bv = ext_in(nc, "bv", [D])
    Wp = ext_in(nc, "Wp", [D, D]); bp = ext_in(nc, "bp", [D])
    rtab = ext_in(nc, "rtab", [32, 16])
    hCOSQ = ext_in(nc, "hCOSQ", [128, T]); hSINQ = ext_in(nc, "hSINQ", [128, T])
    hCOS = ext_in(nc, "hCOS", [128, T]); hSIN = ext_in(nc, "hSIN", [128, T])
    hPERM = ext_in(nc, "hPERM", [128, 128])
    hOH = ext_in(nc, "hOH", [32, 4224])
    hMASK = ext_in(nc, "hMASK", [1, 4224])
    hAID = ext_in(nc, "hAID", [128, 128])
    btab_hbm = nc.dram_tensor("btab", [16, 4224], BF16, kind="Internal").ap()

    with stack:
        alloc_common(ctx, stack, g, nc)
        ATTO = ctx.sb(stack, "ATTO", [128, NCH * T], BF16)
        AID = ctx.sb(stack, "AID", [128, 128], BF16)
        taid = ctx.dma("gpsimd", AID[:], hAID[0:128, 0:128])
        prelude(ctx, g, [])
        emit_btab_prep(ctx, g, stack, rtab, hOH, hMASK, btab_hbm)
        emit_ln(ctx, g, xT)
        with ExitStack() as sta:
            QF = ctx.sb(sta, "QF", [128, NCH * T], BF16)
            KF = ctx.sb(sta, "KF", [128, NCH * T], BF16)
            VA = ctx.sb(sta, "VA", [128, NKT * NH * 65], BF16)
            with ExitStack() as stt:
                COSQ, t1 = load_tab(ctx, stt, "COSQ", hCOSQ, (128, T))
                SINQ, t2 = load_tab(ctx, stt, "SINQ", hSINQ, (128, T))
                COS, t3 = load_tab(ctx, stt, "COS", hCOS, (128, T))
                SIN, t4 = load_tab(ctx, stt, "SIN", hSIN, (128, T))
                PERM, t5 = load_tab(ctx, stt, "PERM", hPERM, (128, 128), dtype=F32)
                for tk in (t1, t2, t3, t4, t5):
                    ctx.wait_dma("vector", tk)
                    ctx.wait_dma("tensor", tk)
                with ExitStack() as stq:
                    wbq, wt = stage_w(ctx, stq, "Wq", Wq)
                    bqc, bt = load_cols(ctx, stq, "bq", bq, D)
                    emit_qk_rope(ctx, g, stq, wbq, wt, bqc, bt, QF, COSQ, SINQ, PERM)
                with ExitStack() as stk:
                    wbk, wt = stage_w(ctx, stk, "Wk", Wk)
                    bkc, bt = load_cols(ctx, stk, "bk", bk, D)
                    emit_qk_rope(ctx, g, stk, wbk, wt, bkc, bt, KF, COS, SIN, PERM)
                with ExitStack() as stv:
                    wbv, wt = stage_w(ctx, stv, "Wv", Wv)
                    bvr, bt = load_row(ctx, stv, "bv", bv, D)
                    emit_v_tok(ctx, g, stv, wbv, wt, bvr, bt, VA)
            ctx.wait_dma("tensor", taid)
            emit_attn_heads(ctx, g, QF, KF, VA, ATTO, expC=15.0,
                            causal=True, skew={"btab_hbm": btab_hbm, "AID": AID})
        with ExitStack() as stp:
            wbp, wt = stage_w(ctx, stp, "Wp", Wp)
            bpc, bt = load_cols(ctx, stp, "bp", bp, D)
            emit_proj_resadd(ctx, g, wbp, wt, bpc, bt, ATTO, xT, yT)
        with nc.Block() as block:
            ctx.finalize(block)
    return nc


def emit_xtok(ctx, g, st, XT, IDB):
    LNX = g["LNX"]
    with ExitStack() as stp:
        psT = ctx.ps(stp, "xtk", [128, 2 * 1024], BF16)
        ev = {}
        i = 0
        for kt in range(NKT):
            for c in range(NCH):
                sl = i % 2
                if i >= 2:
                    ctx.wait("tensor", "c_vec", ev[i - 2])
                pm = ctx.pe(lambda e, kt=kt, c=c, sl=sl: e.transpose(
                    psT[:, sl * 1024:sl * 1024 + 128],
                    LNX[:, c * T + kt * 128:c * T + kt * 128 + 128],
                    IDB[:, :]), inc=True)
                ctx.wait("vector", "c_pe", pm)
                ev[i] = ctx.op("vector", lambda e, kt=kt, c=c, sl=sl: e.tensor_copy(
                    XT[:, kt * 1024 + c * 128:kt * 1024 + c * 128 + 128],
                    psT[:, sl * 1024:sl * 1024 + 128]))
                i += 1
        ctx.barrier()


def build_l1_attn_a(upto="all"):
    nc, stack, ctx, g = new_program()
    xT = ext_in(nc, "xT", [D, T])
    yT = ext_out(nc, "yT", [D, T])
    U = ext_in(nc, "U", [D, 32])
    hID = ext_in(nc, "hID", [128, 128])

    with stack:
        alloc_common(ctx, stack, g, nc)
        IDB = ctx.sb(stack, "IDB", [128, 128], BF16)
        tid = ctx.dma("gpsimd", IDB[:], hID[0:128, 0:128])
        prelude(ctx, g, [])
        emit_ln(ctx, g, xT)
        XT = ctx.sb(stack, "XT", [128, NKT * 1024], BF16)
        ctx.wait_dma("tensor", tid)
        emit_xtok(ctx, g, stack, XT, IDB)
        AUGL = ctx.sb(stack, "AUGL", [64, T], BF16)
        AUGR = ctx.sb(stack, "AUGR", [64, T], BF16)
        with ExitStack() as st:
            wb, wt = stage_w(ctx, st, "U", U, cols=32)
            YB = ctx.sb(st, "YB", [32, T], BF16)
            SQY = ctx.sb(st, "SQY", [32, T], BF16)
            NROW = ctx.sb(st, "NROW", [1, T], F32)
            NROWB = ctx.sb(st, "NROWB", [1, 2 * T], BF16)
            psY = ctx.ps(st, "psY", [32, T])
            psN = ctx.ps(st, "psN", [1, T])
            ctx.wait_dma("tensor", wt)
            pm = 0
            for k in range(4):
                for c in range(NCH):
                    pm = ctx.pe(lambda e, k=k, c=c: e.matmul(
                        psY[0:32, 512 * k:512 * k + 512],
                        wb[:, c * 32:c * 32 + 32],
                        g["LNX"][:, c * T + 512 * k:c * T + 512 * k + 512],
                        start=(c == 0), stop=(c == NCH - 1)),
                        inc=(c == NCH - 1 and k == 3))
            ctx.wait("vector", "c_pe", pm)
            yc = ctx.op("vector", lambda e: e.tensor_scalar_mul(
                YB[0:32, :], psY[0:32, 0:T], 1.4142135623730951))
            ctx.wait("scalar", "c_vec", yc)
            sqc = ctx.op("scalar", lambda e: e.activation(
                out=SQY[0:32, :], in_=psY[0:32, 0:T], func=AF.Square))
            ctx.wait("tensor", "c_scal", sqc)
            pn = 0
            for k in range(4):
                pn = ctx.pe(lambda e, k=k: e.matmul(
                    psN[0:1, 512 * k:512 * k + 512], g["ONESB"][0:32, 0:1],
                    SQY[0:32, 512 * k:512 * k + 512], start=True, stop=True),
                    inc=(k == 3))
            ctx.wait("scalar", "c_pe", pn)
            nc1 = ctx.op("scalar", lambda e: e.copy(NROW[0:1, :], psN[0:1, 0:T]))
            nc2 = ctx.op("scalar", lambda e: e.activation(
                out=NROWB[0:1, 0:T], in_=NROW[0:1, :], func=AF.Copy))
            mneg = ctx.const(-1.0)[0:1, 0:1]
            nc3 = ctx.op("scalar", lambda e: e.activation(
                out=NROWB[0:1, T:2 * T], in_=NROW[0:1, :], func=AF.Copy, scale=mneg))
            zA = ctx.op("vector", lambda e: e.memset(AUGL[:, :], 0.0))
            zB = ctx.op("vector", lambda e: e.memset(AUGR[:, :], 0.0))
            cpA = ctx.op("vector", lambda e: e.tensor_copy(AUGL[0:32, :], YB[0:32, :]))
            cpB = ctx.op("vector", lambda e: e.tensor_copy(AUGR[0:32, :], YB[0:32, :]))
            ONER = ctx.sb(st, "ONER", [1, T], BF16)
            MONER = ctx.sb(st, "MONER", [1, T], BF16)
            o1 = ctx.op("vector", lambda e: e.memset(ONER[0:1, :], 1.0))
            o2 = ctx.op("vector", lambda e: e.memset(MONER[0:1, :], -1.0))
            vnow = ctx.sem("c_vec").n
            snow = ctx.sem("c_scal").n
            d1 = ctx.dma("sync", AUGL[32:33, 0:T], NROWB[0:1, 0:T],
                         after=[("c_vec", vnow), ("c_scal", snow)])
            d2 = ctx.dma("sync", AUGL[33:34, 0:T], ONER[0:1, 0:T],
                         after=[("c_vec", vnow)])
            d3 = ctx.dma("sync", AUGR[32:33, 0:T], MONER[0:1, 0:T],
                         after=[("c_vec", vnow)])
            d4 = ctx.dma("sync", AUGR[33:34, 0:T], NROWB[0:1, T:2 * T],
                         after=[("c_vec", vnow), ("c_scal", snow)])
            for dd in (d1, d2, d3, d4):
                ctx.wait_dma("tensor", dd)
            ctx.barrier()
        with ExitStack() as st:
            psS = ctx.ps(st, "psS", [128, 2 * 512])
            psO = ctx.ps(st, "psO", [128, 512])
            psSum = ctx.ps(st, "psSum", [1, 512])
            psBC = ctx.ps(st, "psBC", [128, 512])
            ATT = ctx.sb(st, "ATT", [128, NKT * 512], BF16)
            SINV = ctx.sb(st, "SINV", [128, 512], F32)
            SSROW = ctx.sb(st, "SSROW", [1, 512], F32)
            STG2 = ctx.sb(st, "STG2", [128, 2 * 512], F32)
            OST = ctx.sb(st, "OST", [128, 2 * 512], F32)
            slot_tok = {}
            for qb in range(NQ):
                ctx.wait("tensor", "c_scal", ctx.sem("c_scal").n)
                exp_cnt = {}
                for kt in range(NKT):
                    sl = kt % 2
                    if kt >= 2:
                        ctx.wait("tensor", "c_scal", exp_cnt[kt - 2])
                    pm = ctx.pe(lambda e, qb=qb, kt=kt, sl=sl: e.matmul(
                        psS[:, sl * 512:sl * 512 + 512],
                        AUGL[:, kt * 128:kt * 128 + 128],
                        AUGR[:, qb * 512:qb * 512 + 512],
                        start=True, stop=True), inc=True)
                    ctx.wait("scalar", "c_pe", pm)
                    exp_cnt[kt] = ctx.op("scalar", lambda e, kt=kt, sl=sl: e.activation(
                        out=ATT[:, kt * 512:kt * 512 + 512],
                        in_=psS[:, sl * 512:sl * 512 + 512], func=AF.Exp))
                ctx.wait("tensor", "c_scal", exp_cnt[NKT - 1])
                ps_ = 0
                for kt in range(NKT):
                    ps_ = ctx.pe(lambda e, kt=kt: e.matmul(
                        psSum[0:1, 0:512], g["ONESB"][:, 0:1],
                        ATT[:, kt * 512:kt * 512 + 512],
                        start=(kt == 0), stop=(kt == NKT - 1)),
                        inc=(kt == NKT - 1))
                ctx.wait("scalar", "c_pe", ps_)
                sscp = ctx.op("scalar", lambda e: e.copy(SSROW[0:1, :], psSum[0:1, 0:512]))
                ctx.wait("tensor", "c_scal", sscp)
                pb = ctx.pe(lambda e: e.matmul(
                    psBC[:, 0:512], g["ONESROWF"][0:1, :], SSROW[0:1, :],
                    start=True, stop=True), inc=True)
                ctx.wait("vector", "c_pe", pb)
                rc = ctx.op("vector", lambda e: e.reciprocal(SINV[:, :], psBC[:, 0:512]))
                for m in range(NCH):
                    aft = [("c_vec", ctx.sem("c_vec").n)] if m >= 2 else []
                    xin = ctx.dma("sync", STG2[:, (m % 2) * 512:(m % 2) * 512 + 512],
                                  xT[m * 128:(m + 1) * 128, qb * 512:qb * 512 + 512],
                                  after=aft)
                    pm = 0
                    for kt in range(NKT):
                        pm = ctx.pe(lambda e, m=m, kt=kt: e.matmul(
                            psO[:, 0:512],
                            XT[:, kt * 1024 + m * 128:kt * 1024 + m * 128 + 128],
                            ATT[:, kt * 512:kt * 512 + 512],
                            start=(kt == 0), stop=(kt == NKT - 1)),
                            inc=(kt == NKT - 1))
                    ctx.wait("vector", "c_pe", pm)
                    ctx.wait_dma("vector", xin)
                    if (m % 2) in slot_tok:
                        ctx.wait_dma("vector", slot_tok[m % 2])
                    t1 = ctx.op("vector", lambda e, m=m: e.scalar_tensor_tensor(
                        out=OST[:, (m % 2) * 512:(m % 2) * 512 + 512],
                        in0=psO[:, 0:512], scalar=1.0, in1=SINV[:, :],
                        op0=ALU.mult, op1=ALU.mult))
                    ev = ctx.op("vector", lambda e, m=m: e.tensor_tensor(
                        OST[:, (m % 2) * 512:(m % 2) * 512 + 512],
                        OST[:, (m % 2) * 512:(m % 2) * 512 + 512],
                        STG2[:, (m % 2) * 512:(m % 2) * 512 + 512], ALU.add))
                    slot_tok[m % 2] = ctx.dma(
                        "sync", yT[m * 128:(m + 1) * 128, qb * 512:qb * 512 + 512],
                        OST[:, (m % 2) * 512:(m % 2) * 512 + 512],
                        after=[("c_vec", ev)])
                    ctx.wait("tensor", "c_vec", t1)
            ctx.wait_dma_all("sync", "sync")
            ctx.barrier()
        with nc.Block() as block:
            ctx.finalize(block)
    return nc


def build_l1_ffn(upto="all"):
    nc, stack, ctx, g = new_program()
    xT = ext_in(nc, "xT", [D, T])
    yT = ext_out(nc, "yT", [D, T])
    gate_w = ext_in(nc, "gate_w", [D, 8])
    gbias = ext_in(nc, "gbias", [8])
    up_w = ext_in(nc, "up_w", [8 * D, 4096])
    up_b = ext_in(nc, "up_b", [8, 4096])
    down_w = ext_in(nc, "down_w", [8 * 4096, D])
    down_b = ext_in(nc, "down_b", [8, D])
    sw_up = ext_in(nc, "sw_up", [D, 4096])
    sb_up = ext_in(nc, "sb_up", [4096])
    sw_down = ext_in(nc, "sw_down", [4096, D])
    sb_down = ext_in(nc, "sb_down", [D])
    hID = ext_in(nc, "hID", [128, 128])
    upbf = nc.dram_tensor("upbf", [9 * D, 4096], BF16, kind="Internal").ap()
    downbf = nc.dram_tensor("downbf", [9 * 4096, D], BF16, kind="Internal").ap()
    whbm = nc.dram_tensor("whbm", [16384], F32, kind="Internal").ap()
    amax_hbm = nc.dram_tensor("amax", [T], F32, kind="Internal").ap()

    NE = 9
    with stack:
        alloc_common(ctx, stack, g, nc, big_ones=True)
        OUT = ctx.sb(stack, "OUT", [128, NCH * T], F32)
        IDF = ctx.sb(stack, "IDF", [128, 128], F32)
        tid = ctx.dma("gpsimd", IDF[:], hID[0:128, 0:128])
        prelude(ctx, g, [])

        PSTG = ctx.sb(stack, "PSTG", [128, 4096], BF16)
        prep_toks = {}
        pslot_tok = {}
        pslot_ctr = [0]

        def prep_expert(e):
            toks = []
            for blk in range(8):
                sl = 0
                if e == 8:
                    src = bass.AP(sw_up.tensor, blk * 128 * 4096,
                                  [[4096, 128], [1, 4096]])
                else:
                    src = bass.AP(up_w.tensor, (e * D + blk * 128) * 4096,
                                  [[4096, 128], [1, 4096]])
                aft = [pslot_tok[sl]] if sl in pslot_tok else []
                tin = ctx.dma("gpsimd", PSTG[:, 0:4096], src, after=aft)
                dst = bass.AP(upbf.tensor, (e * D + blk * 128) * 4096,
                              [[4096, 128], [1, 4096]])
                tout = ctx.dma("sync", dst, PSTG[:, 0:4096], after=[tin])
                pslot_tok[sl] = tout
                toks.append(tout)
            for blk in range(8):
                sl = 0
                if e == 8:
                    src = bass.AP(sw_down.tensor, blk * 512 * D,
                                  [[D, 128], [128 * D, 4], [1, D]])
                else:
                    src = bass.AP(down_w.tensor, (e * 4096 + blk * 512) * D,
                                  [[D, 128], [128 * D, 4], [1, D]])
                aft = [pslot_tok[sl]] if sl in pslot_tok else []
                tin = ctx.dma("gpsimd",
                              PSTG[:, 0:4096].rearrange(
                                  "p (r c) -> p r c", r=4), src, after=aft)
                dst = bass.AP(downbf.tensor, (e * 4096 + blk * 512) * D,
                              [[D, 128], [128 * D, 4], [1, D]])
                tout = ctx.dma("sync",
                               dst, PSTG[:, 0:4096].rearrange(
                                   "p (r c) -> p r c", r=4),
                               after=[tin])
                pslot_tok[sl] = tout
                toks.append(tout)
            prep_toks[e] = toks

        prep_expert(0)
        prep_expert(1)

        emit_ln(ctx, g, xT)

        GT = ctx.sb(stack, "GT", [128, 128], F32)
        W8 = ctx.sb(stack, "W8", [128, 128], F32)
        WROW = ctx.sb(stack, "WROW", [1, T], F32)
        WEB = ctx.sb(stack, "WEB", [128, T], F32)
        RB = ctx.sb(stack, "RB", [128, T], F32)
        with ExitStack() as st:
            gwb, gwt = stage_w(ctx, st, "gw", gate_w, cols=8)
            GBR = ctx.sb(st, "GBR", [1, 8], BF16)
            gbt = ctx.dma("gpsimd", GBR[0:1, 0:8],
                          bass.AP(gbias.tensor, 0, [[8, 1], [1, 8]]))
            M1 = ctx.sb(st, "M1", [128, 4], F32)
            MSK = ctx.sb(st, "MSK", [128, 16], F32)
            TMP8 = ctx.sb(st, "TMP8", [128, 8], F32)
            psG = ctx.ps(st, "psG", [128, 512])
            psW = ctx.ps(st, "psW", [128, 128])
            ctx.wait_dma("tensor", gwt)
            ctx.wait_dma("tensor", gbt)
            gtv = GT[:].rearrange("p (e t) -> p t e", e=8)
            w8v = W8[:].rearrange("p (e t) -> p t e", e=8)
            sg = 0
            for tt in range(NKT):
                ctx.wait("tensor", "c_scal", sg)
                ctx.pe(lambda e, tt=tt: e.matmul(
                    psG[:, 0:8], g["ONESROWB"][0:1, :],
                    GBR[0:1, 0:8], start=True, stop=False))
                pm = 0
                for c in range(NCH):
                    pm = ctx.pe(lambda e, tt=tt, c=c: e.matmul(
                        psG[:, 0:8],
                        g["LNX"][:, c * T + tt * 128:c * T + tt * 128 + 128],
                        gwb[:, c * 8:c * 8 + 8],
                        start=False, stop=(c == NCH - 1)), inc=(c == NCH - 1))
                ctx.wait("scalar", "c_pe", pm)
                sg = ctx.op("scalar", lambda e, tt=tt: e.activation(
                    out=gtv[:, tt:tt + 1, 0:8], in_=psG[:, 0:8], func=AF.Sigmoid))
                ctx.wait("vector", "c_scal", sg)
                ctx.op("vector", lambda e, tt=tt: e.reduce_max(
                    M1[:, 0:1], gtv[:, tt:tt + 1, 0:8], axis=AX.X))
                ctx.op("vector", lambda e, tt=tt: e.tensor_scalar(
                    MSK[:, 0:8], gtv[:, tt:tt + 1, 0:8].rearrange("p o e -> p (o e)"),
                    M1[:, 0:1], None, ALU.is_ge))
                ctx.op("vector", lambda e, tt=tt: e.scalar_tensor_tensor(
                    out=TMP8[:, 0:8], in0=MSK[:, 0:8], scalar=-1e9,
                    in1=gtv[:, tt:tt + 1, 0:8].rearrange("p o e -> p (o e)"),
                    op0=ALU.mult, op1=ALU.add))
                ctx.op("vector", lambda e: e.reduce_max(
                    M1[:, 1:2], TMP8[:, 0:8], axis=AX.X))
                ctx.op("vector", lambda e: e.tensor_tensor(
                    M1[:, 2:3], M1[:, 0:1], M1[:, 1:2], ALU.add))
                ctx.op("vector", lambda e: e.tensor_scalar(
                    M1[:, 2:3], M1[:, 2:3], 1e-6, 10.0, ALU.add, ALU.mult))
                ctx.op("vector", lambda e: e.reciprocal(M1[:, 3:4], M1[:, 2:3]))
                ctx.op("vector", lambda e, tt=tt: e.tensor_scalar(
                    MSK[:, 8:16], gtv[:, tt:tt + 1, 0:8].rearrange("p o e -> p (o e)"),
                    M1[:, 1:2], None, ALU.is_ge))
                ctx.op("vector", lambda e, tt=tt: e.scalar_tensor_tensor(
                    out=w8v[:, tt:tt + 1, 0:8], in0=MSK[:, 8:16], scalar=M1[:, 3:4],
                    in1=gtv[:, tt:tt + 1, 0:8], op0=ALU.mult, op1=ALU.mult))
            vnow = ctx.sem("c_vec").n
            ctx.wait("tensor", "c_vec", vnow)
            pt = ctx.pe(lambda e: e.transpose(psW[:, :], W8[:, :], IDF[:, :]), inc=True)
            ctx.wait("vector", "c_pe", pt)
            wtc = ctx.op("vector", lambda e: e.tensor_copy(MSKW[:], psW[:, :])) \
                if False else None
            WTRS = ctx.sb(st, "WTRS", [128, 128], F32)
            wtc = ctx.op("vector", lambda e: e.tensor_copy(WTRS[:], psW[:, :]))
            wh = ctx.dma("sync", bass.AP(whbm.tensor, 0, [[128, 128], [1, 128]]),
                         WTRS[:], after=[("c_vec", wtc)])
            ctx.wait_dma("sync", wh)
            ctx.barrier()
            if upto == "gate":
                vn = ctx.sem("c_vec").n
                ctx.dma("gpsimd", yT[0:128, 0:128], W8[:, :], after=[("c_vec", vn)])
                ctx.dma("gpsimd", yT[0:128, 128:256], GT[:, :], after=[("c_vec", vn)])
                ctx.wait_dma_all("gpsimd", "gpsimd")
                ctx.barrier()
                with nc.Block() as block:
                    ctx.finalize(block)
                return nc

        for c in range(NCH):
            ctx.op("vector", lambda e, c=c: e.memset(OUT[:, c * T:c * T + T], 0.0))
        with ExitStack() as st:
            HACT = ctx.sb(st, "HACT", [128, 32 * 512], BF16)
            UPW = ctx.sb(st, "UPW", [128, 2 * 8 * 256], BF16)
            DWNW = ctx.sb(st, "DWNW", [128, 32 * 256], BF16)
            UBROW = ctx.sb(st, "UBROW", [1, 4096], BF16)
            DBROW = ctx.sb(st, "DBROW", [1, D], BF16)
            TMPE = ctx.sb(st, "TMPE", [128, 512], F32)
            psU = ctx.ps(st, "psU", [128, 2 * 512])
            psD = ctx.ps(st, "psD", [128, 2 * 512])
            psWB = ctx.ps(st, "psWB", [128, 512])
            up_guard = {0: 0, 1: 0}
            down_guard = {0: ("c_vec", 0)}
            psd_guard = {0: 0, 1: 0}
            web_guard = 0
            NEv = NE if upto == "all" else int(upto[1:].rstrip("x")) if upto.startswith("e") else NE
            elist = list(range(max(0, NEv - 1))) + [8]
            for ee in elist:
                if ee not in prep_toks:
                    prep_expert(ee)
            for e in elist:
                for tok in prep_toks[e]:
                    ctx.wait_dma("tensor", tok)
                bias_aft = [("c_pe", ctx.sem("c_pe").n)]
                if e == 8:
                    ubt = ctx.dma("gpsimd", UBROW[0:1, :],
                                  bass.AP(sb_up.tensor, 0, [[4096, 1], [1, 4096]]),
                                  after=bias_aft)
                    dbt = ctx.dma("gpsimd", DBROW[0:1, :],
                                  bass.AP(sb_down.tensor, 0, [[D, 1], [1, D]]),
                                  after=bias_aft)
                else:
                    ubt = ctx.dma("gpsimd", UBROW[0:1, :],
                                  bass.AP(up_b.tensor, e * 4096, [[4096, 1], [1, 4096]]),
                                  after=bias_aft)
                    dbt = ctx.dma("gpsimd", DBROW[0:1, :],
                                  bass.AP(down_b.tensor, e * D, [[D, 1], [1, D]]),
                                  after=bias_aft)
                ctx.wait_dma("tensor", ubt)
                ctx.wait_dma("tensor", dbt)
                ctx.wait("vector", "c_pe", web_guard)
                if e == 8:
                    wc = ctx.op("vector", lambda e_: e_.memset(WEB[:, :], 0.1))
                else:
                    wr = ctx.dma("sync", WROW[0:1, 0:T],
                                 bass.AP(whbm.tensor, 2048 * e, [[T, 1], [1, T]]))
                    ctx.wait_dma("tensor", wr)
                    pws = 0
                    for k in range(4):
                        if k >= 1:
                            ctx.wait("tensor", "c_vec", wc)
                        pws = ctx.pe(lambda e_, k=k: e_.matmul(
                            psWB[:, 0:512], g["ONESROWF"][0:1, :],
                            WROW[0:1, 512 * k:512 * k + 512],
                            start=True, stop=True), inc=(True))
                        ctx.wait("vector", "c_pe", pws)
                        wc = ctx.op("vector", lambda e_, k=k: e_.tensor_copy(
                            WEB[:, 512 * k:512 * k + 512], psWB[:, 0:512]))
                for tt in range(NQ):
                    gel = {}
                    for ici in range(16):
                        sl = ici % 2
                        src = bass.AP(upbf.tensor, e * D * 4096 + ici * 256,
                                      [[4096, 128], [128 * 4096, 8], [1, 256]])
                        aft = [("c_pe", up_guard[sl])] if up_guard[sl] else []
                        utok = ctx.dma("sync", UPW[:, sl * 2048:sl * 2048 + 2048]
                                       .rearrange("p (c k) -> p c k", c=8), src,
                                       after=aft)
                        ctx.wait_dma("tensor", utok)
                        for half in range(2):
                            ic = 2 * ici + half
                            if ic >= 2:
                                ctx.wait("tensor", "c_scal", gel[ic - 2])
                            ctx.pe(lambda e_, ic=ic: e_.matmul(
                                psU[:, (ic % 2) * 512:(ic % 2) * 512 + 512],
                                UBROW[0:1, ic * 128:ic * 128 + 128],
                                g["ONES512B"][0:1, :],
                                start=True, stop=False))
                            pu = 0
                            for c in range(NCH):
                                pu = ctx.pe(lambda e_, c=c, sl=sl, half=half, tt=tt,
                                            ic=ic: e_.matmul(
                                    psU[:, (ic % 2) * 512:(ic % 2) * 512 + 512],
                                    UPW[:, sl * 2048 + c * 256 + half * 128:
                                        sl * 2048 + c * 256 + half * 128 + 128],
                                    g["LNX"][:, c * T + tt * 512:c * T + tt * 512 + 512],
                                    start=False, stop=(c == NCH - 1)),
                                    inc=(c == NCH - 1))
                            ctx.wait("scalar", "c_pe", pu)
                            gel[ic] = ctx.op("scalar", lambda e_, ic=ic: e_.activation(
                                out=HACT[:, ic * 512:ic * 512 + 512],
                                in_=psU[:, (ic % 2) * 512:(ic % 2) * 512 + 512],
                                func=AF.Gelu))
                        up_guard[sl] = pu
                    for hp in range(4):
                        aft = [down_guard[0]] if down_guard[0][1] else []
                        dtoks = []
                        for q4 in range(4):
                            srcq = bass.AP(downbf.tensor,
                                           (e * 4096 + q4 * 1024) * D + hp * 256,
                                           [[D, 128], [128 * D, 8], [1, 256]])
                            dtoks.append(ctx.dma(
                                "sync",
                                DWNW[:, q4 * 2048:q4 * 2048 + 2048].rearrange(
                                    "p (c k) -> p c k", c=8), srcq, after=aft))
                            aft = []
                        for dt_ in dtoks:
                            ctx.wait_dma("tensor", dt_)
                        for half in range(2):
                            h = 2 * hp + half
                            if psd_guard[h % 2]:
                                ctx.wait("tensor", "c_vec", psd_guard[h % 2])
                            ctx.pe(lambda e_, h=h: e_.matmul(
                                psD[:, (h % 2) * 512:(h % 2) * 512 + 512],
                                DBROW[0:1, h * 128:h * 128 + 128],
                                g["ONES512B"][0:1, :],
                                start=True, stop=False))
                            pd = 0
                            for ic in range(32):
                                need = gel[ic] if (hp == 0 and half == 0) else 0
                                if need:
                                    ctx.wait("tensor", "c_scal", need)
                                pd = ctx.pe(lambda e_, ic=ic, h=h, half=half: e_.matmul(
                                    psD[:, (h % 2) * 512:(h % 2) * 512 + 512],
                                    DWNW[:, ic * 256 + half * 128:ic * 256 + half * 128 + 128],
                                    HACT[:, ic * 512:ic * 512 + 512],
                                    start=False, stop=(ic == 31)),
                                    inc=(ic == 31))
                            ctx.wait("vector", "c_pe", pd)
                            t1 = ctx.op("vector", lambda e_, h=h, tt=tt: e_.scalar_tensor_tensor(
                                out=TMPE[:, 0:512],
                                in0=psD[:, (h % 2) * 512:(h % 2) * 512 + 512],
                                scalar=1.0,
                                in1=WEB[:, tt * 512:tt * 512 + 512],
                                op0=ALU.mult, op1=ALU.mult))
                            ctx.op("vector", lambda e_, h=h, tt=tt: e_.tensor_tensor(
                                OUT[:, h * T + tt * 512:h * T + tt * 512 + 512],
                                OUT[:, h * T + tt * 512:h * T + tt * 512 + 512],
                                TMPE[:, 0:512], ALU.add))
                            psd_guard[h % 2] = t1
                        down_guard[0] = ("c_vec", ctx.sem("c_vec").n)
                web_guard = ctx.sem("c_pe").n
            ctx.barrier()
            if upto.endswith("x"):
                vn = ctx.sem("c_vec").n
                for c in range(NCH):
                    ctx.dma("gpsimd", yT[c * 128:(c + 1) * 128, 0:T],
                            OUT[:, c * T:c * T + T], after=[("c_vec", vn)])
                ctx.wait_dma_all("gpsimd", "gpsimd")
                ctx.barrier()
                with nc.Block() as block:
                    ctx.finalize(block)
                return nc

        with ExitStack() as st:
            AM = ctx.sb(st, "AM", [128, T], F32)
            REDC = ctx.sb(st, "REDC", [128, 128], F32)
            RT = ctx.sb(st, "RT", [128, 128], F32)
            AROW = ctx.sb(st, "AROW", [1, T], F32)
            STG2 = ctx.sb(st, "STG2", [128, 2 * T], F32)
            OST2 = ctx.sb(st, "OST2", [128, 2 * T], F32)
            psT = ctx.ps(st, "psT", [128, 2 * 512])
            psM = ctx.ps(st, "psM", [128, 512])
            ABSC = ctx.sb(st, "ABSC", [128, 2 * T], F32)
            amz = ctx.op("vector", lambda e: e.memset(AM[:, :], 0.0))
            am = amz
            for c in range(NCH):
                sl = c % 2
                if c >= 2:
                    ctx.wait("scalar", "c_vec", am)
                ac = ctx.op("scalar", lambda e, c=c, sl=sl: e.activation(
                    out=ABSC[:, sl * T:sl * T + T], in_=OUT[:, c * T:c * T + T],
                    func=AF.Abs))
                ctx.wait("vector", "c_scal", ac)
                am = ctx.op("vector", lambda e, c=c, sl=sl: e.tensor_tensor(
                    AM[:, :], AM[:, :], ABSC[:, sl * T:sl * T + T], ALU.max))
            rz = ctx.op("vector", lambda e: e.memset(REDC[:, :], 0.0))
            ctx.wait("tensor", "c_vec", rz)
            red_cnt = {}
            for kt in range(NKT):
                sl = kt % 2
                if kt >= 2:
                    ctx.wait("tensor", "c_vec", red_cnt[kt - 2])
                pt = ctx.pe(lambda e, kt=kt, sl=sl: e.transpose(
                    psT[:, sl * 512:sl * 512 + 128],
                    AM[:, kt * 128:kt * 128 + 128], IDF[:, :]), inc=True)
                ctx.wait("vector", "c_pe", pt)
                red_cnt[kt] = ctx.op("vector", lambda e, kt=kt, sl=sl: e.reduce_max(
                    REDC[:, kt:kt + 1], psT[:, sl * 512:sl * 512 + 128], axis=AX.X))
            vnow = ctx.sem("c_vec").n
            ctx.wait("tensor", "c_vec", vnow)
            pt2 = ctx.pe(lambda e: e.transpose(psT[:, 0:128], REDC[:, :], IDF[:, :]),
                         inc=True)
            ctx.wait("vector", "c_pe", pt2)
            rc2 = ctx.op("vector", lambda e: e.tensor_copy(RT[:, :], psT[:, 0:128]))
            ah = ctx.dma("sync", bass.AP(amax_hbm.tensor, 0, [[128, 16], [1, 128]]),
                         RT[0:16, 0:128], after=[("c_vec", rc2)])
            ab = ctx.dma("sync", AROW[0:1, 0:T],
                         bass.AP(amax_hbm.tensor, 0, [[T, 1], [1, T]]),
                         after=[ah])
            ctx.wait_dma("tensor", ab)
            pbm = 0
            rbk = 0
            for k in range(4):
                if k >= 1:
                    ctx.wait("tensor", "c_vec", rbk)
                pbm = ctx.pe(lambda e, k=k: e.matmul(
                    psM[:, 0:512], g["ONESROWF"][0:1, :],
                    AROW[0:1, 512 * k:512 * k + 512], start=True, stop=True),
                    inc=True)
                ctx.wait("vector", "c_pe", pbm)
                rbk = ctx.op("vector", lambda e, k=k: e.tensor_scalar_add(
                    RB[:, 512 * k:512 * k + 512], psM[:, 0:512], 1e-6))
            rbdone = ctx.op("vector", lambda e: e.reciprocal(RB[:, 0:T], RB[:, 0:T]))
            slot_tok = {}
            for m in range(NCH):
                aft = [("c_vec", ctx.sem("c_vec").n)] if m >= 2 else []
                xin = ctx.dma("sync", STG2[:, (m % 2) * T:(m % 2) * T + T],
                              xT[m * 128:(m + 1) * 128, 0:T], after=aft)
                ctx.wait_dma("vector", xin)
                if (m % 2) in slot_tok:
                    ctx.wait_dma("vector", slot_tok[m % 2])
                t1 = ctx.op("vector", lambda e, m=m: e.scalar_tensor_tensor(
                    out=OST2[:, (m % 2) * T:(m % 2) * T + T],
                    in0=OUT[:, m * T:m * T + T], scalar=1.0, in1=RB[:, 0:T],
                    op0=ALU.mult, op1=ALU.mult))
                ev = ctx.op("vector", lambda e, m=m: e.tensor_tensor(
                    OST2[:, (m % 2) * T:(m % 2) * T + T],
                    OST2[:, (m % 2) * T:(m % 2) * T + T],
                    STG2[:, (m % 2) * T:(m % 2) * T + T], ALU.add))
                slot_tok[m % 2] = ctx.dma("sync", yT[m * 128:(m + 1) * 128, 0:T],
                                          OST2[:, (m % 2) * T:(m % 2) * T + T],
                                          after=[("c_vec", ev)])
            ctx.wait_dma_all("sync", "sync")
            ctx.wait_dma_all("sync", "gpsimd")
            ctx.barrier()
        with nc.Block() as block:
            ctx.finalize(block)
    return nc


def bcast_free(ap2d, inner):
    return bass.AP(ap2d.tensor, ap2d.offset, list(ap2d.ap) + [[0, inner]])


def build_l0_ffn(upto="all"):
    nc, stack, ctx, g = new_program()
    xT = ext_in(nc, "xT", [D, T])
    yT = ext_out(nc, "yT", [D, T])
    phi_w = ext_in(nc, "phi_w", [2048, D])
    phi_b = ext_in(nc, "phi_b", [2048])
    ew_local = ext_in(nc, "ew_local", [8 * D, D])
    eb_local = ext_in(nc, "eb_local", [8, D])
    hID = ext_in(nc, "hID", [128, 128])
    xtil = nc.dram_tensor("xtil", [2048, D], F32, kind="Internal").ap()
    xtrs = nc.dram_tensor("xtrs", [256, D], F32, kind="Internal").ap()
    ytin = nc.dram_tensor("ytin", [256, D], BF16, kind="Internal").ap()
    ytab = nc.dram_tensor("ytab", [2048, D], BF16, kind="Internal").ap()
    rg = [list(range(NCORE))]

    with stack:
        alloc_common(ctx, stack, g, nc, lnx=False, big_ones=True)
        g["LNX"] = ctx.sb(stack, "LNX", [128, NCH * T], BF16)
        IDB = ctx.sb(stack, "IDB", [128, 128], BF16)
        tid = ctx.dma("gpsimd", IDB[:], hID[0:128, 0:128])
        PBROW = ctx.sb(stack, "PBROW", [1, 2048], BF16)
        tpb = ctx.dma("gpsimd", PBROW[0:1, :],
                      bass.AP(phi_b.tensor, 0, [[2048, 1], [1, 2048]]))
        I32PS = ctx.sb(stack, "I32PS", [128, 32], BF16)
        I32PTS = ctx.sb(stack, "I32PTS", [32, 128], BF16)
        hI32P = ext_in(nc, "hI32P", [128, 32])
        hI32PT = ext_in(nc, "hI32PT", [32, 128])
        t32a = ctx.dma("gpsimd", I32PS[:], hI32P[0:128, 0:32])
        t32b = ctx.dma("gpsimd", I32PTS[:], hI32PT[0:32, 0:128])
        prelude(ctx, g, [])
        emit_ln(ctx, g, xT)
        expC = ctx.const(-30.0)
        invD_ap = ctx.const(1.0 / D)
        eps_ap = ctx.const(1e-5)
        mhalf_ap = ctx.const(-0.5)
        XT = ctx.sb(stack, "XT", [128, NKT * 1024], BF16)
        ctx.wait_dma("tensor", tid)
        ctx.wait_dma("tensor", tpb)
        ctx.wait_dma("tensor", t32a)
        ctx.wait_dma("tensor", t32b)
        emit_xtok(ctx, g, stack, XT, IDB)
        PHT = ctx.sb(stack, "PHT", [128, NCH * 2048], BF16)
        with ExitStack() as st:
            phn, pht_tok = stage_w(ctx, st, "phn", phi_w, rows=2048)
            psT = ctx.ps(st, "phT", [128, 2 * 1024], BF16)
            ctx.wait_dma("tensor", pht_tok)
            ev = {}
            i = 0
            for r in range(16):
                for c in range(NCH):
                    sl = i % 2
                    if i >= 2:
                        ctx.wait("tensor", "c_vec", ev[i - 2])
                    pm = ctx.pe(lambda e, r=r, c=c, sl=sl: e.transpose(
                        psT[:, sl * 1024:sl * 1024 + 128],
                        phn[:, r * D + c * 128:r * D + c * 128 + 128],
                        IDB[:, :]), inc=True)
                    ctx.wait("vector", "c_pe", pm)
                    ev[i] = ctx.op("vector", lambda e, r=r, c=c, sl=sl: e.tensor_copy(
                        PHT[:, c * 2048 + r * 128:c * 2048 + r * 128 + 128],
                        psT[:, sl * 1024:sl * 1024 + 128]))
                    i += 1
            ctx.barrier()
        with ExitStack() as st:
            DB = ctx.sb(st, "DB", [128, NKT * 2048], BF16)
            ED = ctx.sb(st, "ED", [128, 2048], BF16)
            SD = ctx.sb(st, "SD", [128, 64], F32)
            XEV = ctx.sb(st, "XEV", [128, 2 * 512], F32)
            psL = ctx.ps(st, "psL", [128, 2048])
            psX = ctx.ps(st, "psX", [128, 2 * 512])
            expd = 0
            for tc in range(NKT):
                if expd:
                    ctx.wait("tensor", "c_scal", expd)
                pm = 0
                for eb in range(4):
                    ctx.pe(lambda e, eb=eb: e.matmul(
                        psL[:, eb * 512:eb * 512 + 512], g["ONESROWB"][0:1, :],
                        PBROW[0:1, eb * 512:eb * 512 + 512], start=True, stop=False))
                    for c in range(NCH):
                        pm = ctx.pe(lambda e, tc=tc, eb=eb, c=c: e.matmul(
                            psL[:, eb * 512:eb * 512 + 512],
                            g["LNX"][:, c * T + tc * 128:c * T + tc * 128 + 128],
                            PHT[:, c * 2048 + eb * 512:c * 2048 + eb * 512 + 512],
                            start=False, stop=(c == NCH - 1)), inc=(c == NCH - 1))
                ctx.wait("scalar", "c_pe", pm)
                if tc >= 1:
                    ctx.wait("scalar", "c_vec", dbn)
                expd = ctx.op("scalar", lambda e, b=expC: e.activation(
                    out=ED[:, 0:2048], in_=psL[:, 0:2048], func=AF.Exp, bias=b))
                ctx.wait("vector", "c_scal", expd)
                ctx.op("vector", lambda e: e.reduce_sum(
                    SD[:, 0:64], ED[:, 0:2048].rearrange("p (n q) -> p n q", n=64),
                    axis=AX.X))
                ctx.op("vector", lambda e: e.reciprocal(SD[:, 0:64], SD[:, 0:64]))
                dbn = ctx.op("vector", lambda e, tc=tc: e.tensor_tensor(
                    DB[:, tc * 2048:tc * 2048 + 2048].rearrange(
                        "p (n q) -> p n q", n=64),
                    ED[:, 0:2048].rearrange("p (n q) -> p n q", n=64),
                    bcast_free(SD[:, 0:64], 32), ALU.mult))
                dbn = dbn
            ctx.wait("tensor", "c_vec", ctx.sem("c_vec").n)
            xev_tok = {}
            for npc in range(16):
                for dt_ in range(2):
                    sl = dt_
                    pm = 0
                    for tc in range(NKT):
                        pm = ctx.pe(lambda e, npc=npc, dt_=dt_, tc=tc, sl=sl: e.matmul(
                            psX[:, sl * 512:sl * 512 + 512],
                            DB[:, tc * 2048 + npc * 128:tc * 2048 + npc * 128 + 128],
                            XT[:, tc * 1024 + dt_ * 512:tc * 1024 + dt_ * 512 + 512],
                            start=(tc == 0), stop=(tc == NKT - 1)),
                            inc=(tc == NKT - 1))
                    ctx.wait("vector", "c_pe", pm)
                    if sl in xev_tok:
                        ctx.wait_dma("vector", xev_tok[sl])
                    cvx = ctx.op("vector", lambda e, sl=sl: e.tensor_copy(
                        XEV[:, sl * 512:sl * 512 + 512],
                        psX[:, sl * 512:sl * 512 + 512]))
                    xev_tok[sl] = ctx.dma(
                        "sync",
                        xtil[npc * 128:(npc + 1) * 128, dt_ * 512:dt_ * 512 + 512],
                        XEV[:, sl * 512:sl * 512 + 512], after=[("c_vec", cvx)])
                    ctx.wait("tensor", "c_vec", cvx)
            ctx.wait_dma_all("sync", "sync")
            ctx.barrier()
        ctx.op("gpsimd", lambda e: e.collective_compute(
            "ReduceScatter", ALU.add, replica_groups=rg,
            ins=[xtil.opt()], outs=[xtrs.opt()]))
        ctx.wait("tensor", "c_pool", ctx.sem("c_pool").n)
        ctx.barrier()
        with ExitStack() as st:
            XC = ctx.sb(st, "XC", [128, 2 * 1024], BF16)
            XCT = ctx.sb(st, "XCT", [128, 8 * 256], BF16)
            RS = ctx.sb(st, "RS", [128, 2 * 1024], F32)
            SST = ctx.sb(st, "SST", [128, 8], F32)
            EBR = ctx.sb(st, "EBR", [1, D], BF16)
            EWB = ctx.sb(st, "EWB", [128, D], BF16)
            YEV = ctx.sb(st, "YEV", [32, 1024], BF16)
            psYt = ctx.ps(st, "psYt", [32, 1024])
            for half in range(2):
                tok = ctx.dma("sync", RS[:, half * 1024:half * 1024 + 1024],
                              xtrs[half * 128:half * 128 + 128, 0:D])
                ctx.wait_dma("vector", tok)
                ctx.op("vector", lambda e, half=half: e.tensor_scalar(
                    RS[:, half * 1024:half * 1024 + 1024],
                    RS[:, half * 1024:half * 1024 + 1024],
                    -33000.0, 65000.0, ALU.max, ALU.min))
                ctx.op("vector", lambda e, half=half: e.reduce_sum(
                    SST[:, 0:1], RS[:, half * 1024:half * 1024 + 1024], axis=AX.X))
                ctx.op("vector", lambda e: e.tensor_scalar_mul(
                    SST[:, 0:1], SST[:, 0:1], 1.0 / D))
                cen = ctx.op("vector", lambda e, half=half: e.tensor_scalar(
                    RS[:, half * 1024:half * 1024 + 1024],
                    RS[:, half * 1024:half * 1024 + 1024],
                    SST[:, 0:1], None, ALU.subtract))
                ctx.wait("scalar", "c_vec", cen)
                sq = ctx.op("scalar", lambda e, half=half: e.activation(
                    out=XC[:, half * 1024:half * 1024 + 1024],
                    in_=RS[:, half * 1024:half * 1024 + 1024], func=AF.Square))
                ctx.wait("vector", "c_scal", sq)
                vr = ctx.op("vector", lambda e, half=half: e.reduce_sum(
                    SST[:, 1:2], XC[:, half * 1024:half * 1024 + 1024], axis=AX.X))
                ctx.wait("scalar", "c_vec", vr)
                ctx.op("scalar", lambda e, s=invD_ap, b=eps_ap: e.activation(
                    out=SST[:, 2:3], in_=SST[:, 1:2], func=AF.Ln, scale=s, bias=b))
                l2 = ctx.op("scalar", lambda e, s=mhalf_ap: e.activation(
                    out=SST[:, 3:4], in_=SST[:, 2:3], func=AF.Exp, scale=s))
                ctx.wait("vector", "c_scal", l2)
                ap_ = ctx.op("vector", lambda e, half=half: e.tensor_scalar(
                    RS[:, half * 1024:half * 1024 + 1024],
                    RS[:, half * 1024:half * 1024 + 1024],
                    SST[:, 3:4], None, ALU.mult))
                ctx.wait("scalar", "c_vec", ap_)
                ctx.op("scalar", lambda e, half=half: e.activation(
                    out=XC[:, half * 1024:half * 1024 + 1024],
                    in_=RS[:, half * 1024:half * 1024 + 1024], func=AF.Tanh))
            ctx.wait("tensor", "c_scal", ctx.sem("c_scal").n)
            psTT = ctx.ps(st, "psTT", [128, 2 * 1024], BF16)
            ev = {}
            i = 0
            for half in range(2):
                for c in range(NCH):
                    sl = i % 2
                    if i >= 2:
                        ctx.wait("tensor", "c_vec", ev[i - 2])
                    pm = ctx.pe(lambda e, half=half, c=c, sl=sl: e.transpose(
                        psTT[:, sl * 1024:sl * 1024 + 128],
                        XC[:, half * 1024 + c * 128:half * 1024 + c * 128 + 128],
                        IDB[:, :]), inc=True)
                    ctx.wait("vector", "c_pe", pm)
                    ev[i] = ctx.op("vector", lambda e, half=half, c=c, sl=sl: e.tensor_copy(
                        XCT[:, c * 256 + half * 128:c * 256 + half * 128 + 128],
                        psTT[:, sl * 1024:sl * 1024 + 128]))
                    i += 1
            vlast = ctx.sem("c_vec").n
            ctx.wait("tensor", "c_vec", vlast)
            ylast = {}
            for n in range(8):
                ebt = ctx.dma("gpsimd", EBR[0:1, :],
                              bass.AP(eb_local.tensor, n * D, [[D, 1], [1, D]]),
                              after=([("c_pe", ylast[n - 1])] if n else []))
                ctx.wait_dma("tensor", ebt)
                if n:
                    ctx.wait("tensor", "c_vec", yclast)
                ctx.pe(lambda e, n=n: e.matmul(
                    psYt[0:32, 0:512], g["ONESROWB"][0:1, 0:32],
                    EBR[0:1, 0:512], start=True, stop=False))
                ctx.pe(lambda e, n=n: e.matmul(
                    psYt[0:32, 512:1024], g["ONESROWB"][0:1, 0:32],
                    EBR[0:1, 512:1024], start=True, stop=False))
                pm = 0
                for c in range(NCH):
                    ewt = ctx.dma("gpsimd", EWB[:],
                                  ew_local[n * D + c * 128:n * D + c * 128 + 128, 0:D],
                                  after=([("c_pe", pm)] if pm else
                                         ([("c_pe", ylast[n - 1])] if n else [])))
                    ctx.wait_dma("tensor", ewt)
                    for ht in range(2):
                        pm = ctx.pe(lambda e, n=n, c=c, ht=ht: e.matmul(
                            psYt[0:32, ht * 512:ht * 512 + 512],
                            XCT[:, c * 256 + n * 32:c * 256 + n * 32 + 32],
                            EWB[:, ht * 512:ht * 512 + 512],
                            start=False, stop=(c == NCH - 1)),
                            inc=(ht == 1))
                ylast[n] = pm
                ctx.wait("vector", "c_pe", pm)
                yc = ctx.op("vector", lambda e, n=n: e.tensor_copy(
                    YEV[0:32, :], psYt[0:32, 0:1024]))
                yclast = yc
                ytok = ctx.dma("sync", ytin[n * 32:(n + 1) * 32, 0:D], YEV[0:32, :],
                               after=[("c_vec", yc)])
                ctx.wait_dma("vector", ytok)
            ctx.barrier()
        ctx.op("gpsimd", lambda e: e.collective_compute(
            "AllGather", ALU.bypass, replica_groups=rg,
            ins=[ytin.opt()], outs=[ytab.opt()]))
        ctx.wait("tensor", "c_pool", ctx.sem("c_pool").n)
        ctx.barrier()
        with ExitStack() as st:
            ECH = ctx.sb(st, "ECH", [128, 512], BF16)
            CBT = ctx.sb(st, "CBT", [128, 16 * 512], BF16)
            RCB = ctx.sb(st, "RCB", [128, 512], F32)
            YTB = ctx.sb(st, "YTB", [128, 2 * 1024], BF16)
            STG2 = ctx.sb(st, "STG2", [128, 2 * 512], F32)
            OST2 = ctx.sb(st, "OST2", [128, 2 * 512], F32)
            psC = ctx.ps(st, "psC", [128, 2 * 512])
            psS = ctx.ps(st, "psSC", [32, 512])
            psB = ctx.ps(st, "psBB", [128, 512])
            psY2 = ctx.ps(st, "psY2", [128, 2 * 512])
            slot_tok = {}
            for tq in range(NQ):
                ec_cnt = {}
                for npc in range(16):
                    sl = npc % 2
                    if npc >= 2:
                        ctx.wait("tensor", "c_scal", ec_cnt[npc - 2])
                    pm = 0
                    for c in range(NCH):
                        pm = ctx.pe(lambda e, npc=npc, c=c, tq=tq, sl=sl: e.matmul(
                            psC[:, sl * 512:sl * 512 + 512],
                            PHT[:, c * 2048 + npc * 128:c * 2048 + npc * 128 + 128],
                            g["LNX"][:, c * T + tq * 512:c * T + tq * 512 + 512],
                            start=(c == 0), stop=False), inc=False)
                    pm = ctx.pe(lambda e, npc=npc, sl=sl: e.matmul(
                        psC[:, sl * 512:sl * 512 + 512],
                        PBROW[0:1, npc * 128:npc * 128 + 128],
                        g["ONES512B"][0:1, :], start=False, stop=True), inc=True)
                    ctx.wait("scalar", "c_pe", pm)
                    ec_cnt[npc] = ctx.op("scalar", lambda e, npc=npc, sl=sl, b=expC:
                                         e.activation(
                        out=CBT[:, npc * 512:npc * 512 + 512],
                        in_=psC[:, sl * 512:sl * 512 + 512], func=AF.Exp, bias=b))
                    ctx.wait("tensor", "c_scal", ec_cnt[npc])
                    ctx.pe(lambda e, npc=npc: e.matmul(
                        psS[0:32, 0:512], I32PS[:, 0:32],
                        CBT[:, npc * 512:npc * 512 + 512],
                        start=(npc == 0), stop=(npc == 15)), inc=(npc == 15))
                pcs = ctx.sem("c_pe").n
                ctx.wait("scalar", "c_pe", pcs)
                csr = ctx.op("scalar", lambda e: e.activation(
                    out=ECH[0:32, 0:512], in_=psS[0:32, 0:512], func=AF.Copy))
                ctx.wait("tensor", "c_scal", csr)
                pb = ctx.pe(lambda e: e.matmul(
                    psB[:, 0:512], I32PTS[0:32, :], ECH[0:32, 0:512],
                    start=True, stop=True), inc=True)
                ctx.wait("vector", "c_pe", pb)
                rcb = ctx.op("vector", lambda e: e.reciprocal(RCB[:, :], psB[:, 0:512]))
                for npc in range(16):
                    ctx.op("vector", lambda e, npc=npc: e.tensor_tensor(
                        CBT[:, npc * 512:npc * 512 + 512],
                        CBT[:, npc * 512:npc * 512 + 512], RCB[:, :], ALU.mult))
                vnorm = ctx.sem("c_vec").n
                ctx.wait("tensor", "c_vec", vnorm)
                for hc in range(NCH):
                    sl = hc % 2
                    aft = [("c_vec", ctx.sem("c_vec").n)] if sl in slot_tok else []
                    xin = ctx.dma("sync", STG2[:, sl * 512:sl * 512 + 512],
                                  xT[hc * 128:(hc + 1) * 128, tq * 512:tq * 512 + 512],
                                  after=aft)
                    pm = 0
                    for npc in range(16):
                        ywt = ctx.dma("sync",
                                      YTB[:, sl * 1024 + (npc % 2) * 128:
                                          sl * 1024 + (npc % 2) * 128 + 128],
                                      ytab[npc * 128:(npc + 1) * 128,
                                           hc * 128:(hc + 1) * 128],
                                      after=([("c_pe", pm)] if pm else []))
                        ctx.wait_dma("tensor", ywt)
                        pm = ctx.pe(lambda e, npc=npc, hc=hc, sl=sl: e.matmul(
                            psY2[:, sl * 512:sl * 512 + 512],
                            YTB[:, sl * 1024 + (npc % 2) * 128:
                                sl * 1024 + (npc % 2) * 128 + 128],
                            CBT[:, npc * 512:npc * 512 + 512],
                            start=(npc == 0), stop=(npc == 15)),
                            inc=True)
                    ctx.wait("vector", "c_pe", pm)
                    ctx.wait_dma("vector", xin)
                    if sl in slot_tok:
                        ctx.wait_dma("vector", slot_tok[sl])
                    ev2 = ctx.op("vector", lambda e, hc=hc, sl=sl: e.tensor_tensor(
                        OST2[:, sl * 512:sl * 512 + 512],
                        psY2[:, sl * 512:sl * 512 + 512],
                        STG2[:, sl * 512:sl * 512 + 512], ALU.add))
                    slot_tok[sl] = ctx.dma(
                        "sync", yT[hc * 128:(hc + 1) * 128, tq * 512:tq * 512 + 512],
                        OST2[:, sl * 512:sl * 512 + 512], after=[("c_vec", ev2)])
            ctx.wait_dma_all("sync", "sync")
            ctx.barrier()
        with nc.Block() as block:
            ctx.finalize(block)
    return nc


def emit_wt_prep(ctx, g, st, w_hbm, IDB, name):
    WT = ctx.sb(st, name, [128, NCH * D], BF16)
    with ExitStack() as stp:
        wn, tok = stage_w(ctx, stp, name + "n", w_hbm)
        psT = ctx.ps(stp, "wtT", [128, 2 * 1024], BF16)
        ctx.wait_dma("tensor", tok)
        ev = {}
        i = 0
        for m in range(NCH):
            for c in range(NCH):
                sl = i % 2
                if i >= 2:
                    ctx.wait("tensor", "c_vec", ev[i - 2])
                pm = ctx.pe(lambda e, m=m, c=c, sl=sl: e.transpose(
                    psT[:, sl * 1024:sl * 1024 + 128],
                    wn[:, m * D + c * 128:m * D + c * 128 + 128], IDB[:, :]),
                    inc=True)
                ctx.wait("vector", "c_pe", pm)
                ev[i] = ctx.op("vector", lambda e, m=m, c=c, sl=sl: e.tensor_copy(
                    WT[:, c * D + m * 128:c * D + m * 128 + 128],
                    psT[:, sl * 1024:sl * 1024 + 128]))
                i += 1
        ctx.barrier()
    return WT


def emit_proj_plain(ctx, g, wb, wtgt, out_sb):
    LNX = g["LNX"]
    with ExitStack() as st:
        ps = [ctx.ps(st, "ppA", [128, T]), ctx.ps(st, "ppB", [128, T])]
        ctx.wait_dma("tensor", wtgt)
        ev = {}
        for m in range(NCH):
            pcur = ps[m % 2]
            if m >= 2:
                ctx.wait("tensor", "c_scal", ev[m - 2])
            pm = 0
            for k in range(4):
                for c in range(NCH):
                    pm = ctx.pe(lambda e, m=m, k=k, c=c, p=pcur: e.matmul(
                        p[:, 512 * k:512 * k + 512],
                        wb[:, c * D + m * 128:c * D + m * 128 + 128],
                        LNX[:, c * T + 512 * k:c * T + 512 * k + 512],
                        start=(c == 0), stop=(c == NCH - 1)),
                        inc=(c == NCH - 1 and k == 3))
            ctx.wait("scalar", "c_pe", pm)
            ev[m] = ctx.op("scalar", lambda e, m=m, p=pcur: e.activation(
                out=out_sb[:, m * T:m * T + T], in_=p[:, 0:T], func=AF.Copy))
        ctx.wait("tensor", "c_scal", ev[NCH - 1])
        ctx.barrier()


def build_l0_attn_b(upto="all"):
    nc, stack, ctx, g = new_program()
    xT = ext_in(nc, "xT", [D, T])
    yT = ext_out(nc, "yT", [D, T])
    Wq = ext_in(nc, "Wq", [D, D])
    Wk = ext_in(nc, "Wk", [D, D])
    Wv = ext_in(nc, "Wv", [D, D])
    Wr = ext_in(nc, "Wr", [D, D])
    u_bias = ext_in(nc, "u_bias", [1024])
    v_bias = ext_in(nc, "v_bias", [1024])
    hRB = ext_in(nc, "hRB", [1024, T])
    hQC = ext_in(nc, "hQC", [512, T])
    hQS = ext_in(nc, "hQS", [512, T])
    hID = ext_in(nc, "hID", [128, 128])
    aout = nc.dram_tensor("aout", [D, T], BF16, kind="Internal").ap()

    with stack:
        alloc_common(ctx, stack, g, nc, lnx=False)
        QF = ctx.sb(stack, "QF", [128, NCH * T], BF16)
        KF = ctx.sb(stack, "KF", [128, NCH * T], BF16)
        VA = ctx.sb(stack, "VA", [128, NKT * NH * 65], BF16)
        UCOL = ctx.sb(stack, "UCOL", [128, 8], F32)
        VCOL = ctx.sb(stack, "VCOL", [128, 8], F32)
        IDB = ctx.sb(stack, "IDB", [128, 128], BF16)
        tid = ctx.dma("gpsimd", IDB[:], hID[0:128, 0:128])
        tuc = ctx.dma("gpsimd", UCOL[:],
                      bass.AP(u_bias.tensor, 0, [[1, 128], [128, 8]]),
                      allow_slow_non_contiguous=True)
        tvc = ctx.dma("gpsimd", VCOL[:],
                      bass.AP(v_bias.tensor, 0, [[1, 128], [128, 8]]),
                      allow_slow_non_contiguous=True)
        prelude(ctx, g, [])
        expC_ap = ctx.const(-30.0)
        with ExitStack() as lst:
            g["LNX"] = ctx.sb(lst, "LNX", [128, NCH * T], BF16)
            emit_ln(ctx, g, xT)
            ctx.wait_dma("tensor", tid)
            with ExitStack() as st:
                WQT = emit_wt_prep(ctx, g, st, Wq, IDB, "WQT")
                emit_proj_plain(ctx, g, WQT, ("d_sync0", 0), QF)
            with ExitStack() as st:
                WKT = emit_wt_prep(ctx, g, st, Wk, IDB, "WKT")
                emit_proj_plain(ctx, g, WKT, ("d_sync0", 0), KF)
            with ExitStack() as st:
                WVT = emit_wt_prep(ctx, g, st, Wv, IDB, "WVT")
                ZROW = ctx.sb(st, "ZROW", [1, D], F32)
                zr = ctx.op("vector", lambda e: e.memset(ZROW[0:1, :], 0.0))
                ctx.wait("tensor", "c_vec", zr)
                emit_v_tok(ctx, g, st, WVT, ("d_sync0", 0), ZROW, ("d_sync0", 0), VA)
        with ExitStack() as st:
            RBT = ctx.sb(st, "RBT", [128, NCH * T], BF16)
            trb = ctx.dma("gpsimd", RBT[:].rearrange("p (c t) -> p c t", c=NCH),
                          bass.AP(hRB.tensor, 0, [[T, 128], [128 * T, 8], [1, T]]))
            WRB_, twr = stage_w(ctx, st, "Wr", Wr)
            CQT = ctx.sb(st, "CQT", [128, 4 * 512], BF16)
            SQT = ctx.sb(st, "SQT", [128, 4 * 512], BF16)
            QU = ctx.sb(st, "QU", [128, 512], BF16)
            QV = ctx.sb(st, "QV", [128, 512], BF16)
            VB = ctx.sb(st, "VB", [128, 8 * 512], BF16)
            AB = ctx.sb(st, "AB", [128, 8 * 512], BF16)
            TM1 = ctx.sb(st, "TM1", [128, 512], BF16)
            ATT = ctx.sb(st, "ATT", [128, 2 * 512], BF16)
            SINV = ctx.sb(st, "SINV", [128, 512], F32)
            SSROW = ctx.sb(st, "SSROW", [1, 512], F32)
            OEV = ctx.sb(st, "OEV", [128, 512], BF16)
            psS = ctx.ps(st, "psS", [128, 2 * 512])
            psO = ctx.ps(st, "psO", [128, 512])
            psBC = ctx.ps(st, "psBC", [128, 512])
            psV = ctx.ps(st, "psVb", [128, 2 * 512])
            for tk in (trb, twr, tuc, tvc):
                ctx.wait_dma("tensor", tk)
                ctx.wait_dma("vector", tk)
            out_tok = {}
            tab_tok = {}
            for qb in range(NQ):
                vnow = ctx.sem("c_vec").n
                tab_tok[0] = ctx.dma(
                    "gpsimd", CQT[:].rearrange("p (c t) -> p c t", c=4),
                    bass.AP(hQC.tensor, qb * 512, [[T, 128], [128 * T, 4], [1, 512]]),
                    after=[("c_vec", vnow)])
                tab_tok[1] = ctx.dma(
                    "gpsimd", SQT[:].rearrange("p (c t) -> p c t", c=4),
                    bass.AP(hQS.tensor, qb * 512, [[T, 128], [128 * T, 4], [1, 512]]),
                    after=[("c_vec", vnow)])
                ctx.wait_dma("vector", tab_tok[0])
                ctx.wait_dma("vector", tab_tok[1])
                for h in range(NH):
                    ch = h // 2
                    base = 64 * (h % 2)
                    ctx.wait("vector", "c_pe", ctx.sem("c_pe").n)
                    uq = ctx.op("vector", lambda e, ch=ch, base=base, qb=qb: e.tensor_scalar(
                        QU[base:base + 64, 0:512],
                        QF[base:base + 64, ch * T + qb * 512:ch * T + qb * 512 + 512],
                        UCOL[base:base + 64, ch:ch + 1], 0.125, ALU.add, ALU.mult))
                    vq = ctx.op("vector", lambda e, ch=ch, base=base, qb=qb: e.tensor_scalar(
                        QV[base:base + 64, 0:512],
                        QF[base:base + 64, ch * T + qb * 512:ch * T + qb * 512 + 512],
                        VCOL[base:base + 64, ch:ch + 1], 0.125, ALU.add, ALU.mult))
                    ctx.wait("tensor", "c_vec", max(uq, vq))
                    vev = {}
                    for c in range(NCH):
                        sl = c % 2
                        if c >= 2:
                            ctx.wait("tensor", "c_vec", vev[c - 2])
                        pm = ctx.pe(lambda e, h=h, c=c, sl=sl, base=base: e.matmul(
                            psV[:, sl * 512:sl * 512 + 512],
                            WRB_[(h % 2) * 64:(h % 2) * 64 + 64,
                                 (h // 2) * D + c * 128:(h // 2) * D + c * 128 + 128],
                            QV[base:base + 64, 0:512],
                            start=True, stop=True), inc=True)
                        ctx.wait("vector", "c_pe", pm)
                        vev[c] = ctx.op("vector", lambda e, c=c, sl=sl: e.tensor_copy(
                            VB[:, c * 512:c * 512 + 512],
                            psV[:, sl * 512:sl * 512 + 512]))
                    ab_cnt = 0
                    for c in range(4):
                        cqs = CQT[:, c * 512:c * 512 + 512]
                        sqs = SQT[:, c * 512:c * 512 + 512]
                        ctx.op("vector", lambda e, c=c, cqs=cqs: e.tensor_tensor(
                            TM1[:, 0:512], VB[:, c * 512:c * 512 + 512], cqs, ALU.mult))
                        ctx.op("vector", lambda e, c=c, sqs=sqs: e.tensor_tensor(
                            AB[:, c * 512:c * 512 + 512],
                            VB[:, (c + 4) * 512:(c + 4) * 512 + 512], sqs, ALU.mult))
                        ctx.op("vector", lambda e, c=c: e.tensor_tensor(
                            AB[:, c * 512:c * 512 + 512], TM1[:, 0:512],
                            AB[:, c * 512:c * 512 + 512], ALU.add))
                        ctx.op("vector", lambda e, c=c, cqs=cqs: e.tensor_tensor(
                            TM1[:, 0:512], VB[:, (c + 4) * 512:(c + 4) * 512 + 512],
                            cqs, ALU.mult))
                        ctx.op("vector", lambda e, c=c, sqs=sqs: e.tensor_tensor(
                            AB[:, (c + 4) * 512:(c + 4) * 512 + 512],
                            VB[:, c * 512:c * 512 + 512], sqs, ALU.mult))
                        ab_cnt = ctx.op("vector", lambda e, c=c: e.tensor_tensor(
                            AB[:, (c + 4) * 512:(c + 4) * 512 + 512], TM1[:, 0:512],
                            AB[:, (c + 4) * 512:(c + 4) * 512 + 512], ALU.subtract))
                    ctx.wait("tensor", "c_vec", ab_cnt)
                    exp_cnt = {}
                    av_cnt = {}
                    ctx.wait("tensor", "c_scal", ctx.sem("c_scal").n)

                    def emit_av(kt):
                        ctx.wait("tensor", "c_scal", exp_cnt[kt])
                        av_cnt[kt] = ctx.pe(lambda e, h=h, kt=kt: e.matmul(
                            psO[0:65, 0:512],
                            VA[:, kt * NH * 65 + h * 65:kt * NH * 65 + h * 65 + 65],
                            ATT[:, (kt % 2) * 512:(kt % 2) * 512 + 512],
                            start=(kt == 0), stop=(kt == NKT - 1)),
                            inc=(kt == NKT - 1))

                    for kt in range(NKT):
                        sl = kt % 2
                        if kt >= 2:
                            ctx.wait("tensor", "c_scal", exp_cnt[kt - 2])
                            ctx.wait("scalar", "c_pe", av_cnt[kt - 2])
                        ctx.pe(lambda e, ch=ch, base=base, kt=kt, sl=sl: e.matmul(
                            psS[:, sl * 512:sl * 512 + 512],
                            KF[base:base + 64, ch * T + kt * 128:ch * T + kt * 128 + 128],
                            QU[base:base + 64, 0:512],
                            start=True, stop=False))
                        pm = 0
                        for c in range(NCH):
                            pm = ctx.pe(lambda e, c=c, kt=kt, sl=sl: e.matmul(
                                psS[:, sl * 512:sl * 512 + 512],
                                RBT[:, c * T + kt * 128:c * T + kt * 128 + 128],
                                AB[:, c * 512:c * 512 + 512],
                                start=False, stop=(c == NCH - 1)),
                                inc=(c == NCH - 1))
                        ctx.wait("scalar", "c_pe", pm)
                        exp_cnt[kt] = ctx.op("scalar", lambda e, kt=kt, sl=sl, b=expC_ap:
                                             e.activation(
                            out=ATT[:, (kt % 2) * 512:(kt % 2) * 512 + 512],
                            in_=psS[:, sl * 512:sl * 512 + 512], func=AF.Exp,
                            bias=b))
                        if kt >= 1:
                            emit_av(kt - 1)
                    emit_av(NKT - 1)
                    ctx.wait("scalar", "c_pe", av_cnt[NKT - 1])
                    sscp = ctx.op("scalar", lambda e: e.copy(SSROW[0:1, :],
                                                             psO[64:65, 0:512]))
                    ctx.wait("tensor", "c_scal", sscp)
                    pb = ctx.pe(lambda e: e.matmul(
                        psBC[0:64, 0:512], g["ONESROWF"][0:1, 0:64], SSROW[0:1, :],
                        start=True, stop=True), inc=True)
                    ctx.wait("vector", "c_pe", pb)
                    ctx.op("vector", lambda e: e.reciprocal(SINV[0:64, :],
                                                            psBC[0:64, 0:512]))
                    prev = qb * NH + h - 1
                    if prev >= 0 and prev in out_tok:
                        ctx.wait_dma("vector", out_tok[prev])
                    ec = ctx.op("vector", lambda e: e.tensor_tensor(
                        OEV[0:64, 0:512], psO[0:64, 0:512], SINV[0:64, :], ALU.mult))
                    out_tok[qb * NH + h] = ctx.dma(
                        "sync", aout[h * 64:h * 64 + 64, qb * 512:qb * 512 + 512],
                        OEV[0:64, 0:512], after=[("c_vec", ec)])
                    ctx.wait("tensor", "c_vec", ec)
            ctx.wait_dma_all("sync", "sync")
            ctx.barrier()
        with ExitStack() as st:
            STG2 = ctx.sb(st, "STG2", [128, 2 * T], F32)
            AOB = ctx.sb(st, "AOB", [128, 2 * T], BF16)
            OST2 = ctx.sb(st, "OST2", [128, 2 * T], F32)
            slot_tok = {}
            for m in range(NCH):
                sl = m % 2
                aft = [("c_vec", ctx.sem("c_vec").n)] if sl in slot_tok else []
                xin = ctx.dma("sync", STG2[:, sl * T:sl * T + T],
                              xT[m * 128:(m + 1) * 128, 0:T], after=aft)
                ain = ctx.dma("sync", AOB[:, sl * T:sl * T + T],
                              aout[m * 128:(m + 1) * 128, 0:T], after=aft)
                ctx.wait_dma("vector", xin)
                ctx.wait_dma("vector", ain)
                if sl in slot_tok:
                    ctx.wait_dma("vector", slot_tok[sl])
                ev = ctx.op("vector", lambda e, sl=sl: e.tensor_tensor(
                    OST2[:, sl * T:sl * T + T], STG2[:, sl * T:sl * T + T],
                    AOB[:, sl * T:sl * T + T], ALU.add))
                slot_tok[sl] = ctx.dma("sync", yT[m * 128:(m + 1) * 128, 0:T],
                                       OST2[:, sl * T:sl * T + T],
                                       after=[("c_vec", ev)])
            ctx.wait_dma_all("sync", "sync")
            ctx.barrier()
        with nc.Block() as block:
            ctx.finalize(block)
    return nc


_cache = {}


def _get(name, builder):
    if name not in _cache:
        _cache[name] = builder()
    return _cache[name]


def run_l1_attn_b(xT_np, Wq, bq, Wk, bk, Wv, bv, Wp, bp, upto="all"):
    t = host_tables()
    Wqp, bqp = permute_head_evens_odds(Wq, bq)
    Wkp, bkp = permute_head_evens_odds(Wk, bk)
    nc = _get("l1b_" + upto, lambda: build_l1_attn_b(upto))
    in_maps = []
    for c in range(NCORE):
        in_maps.append({
            "xT": np.ascontiguousarray(xT_np[c]),
            "Wq": Wqp, "bq": bqp, "Wk": Wkp, "bk": bkp,
            "Wv": np.ascontiguousarray(Wv), "bv": np.ascontiguousarray(bv),
            "Wp": np.ascontiguousarray(Wp), "bp": np.ascontiguousarray(bp),
            "hCOSQ": t["RCOSQ"], "hSINQ": t["RSINQ"],
            "hCOS": t["RCOS"], "hSIN": t["RSIN"], "hPERM": t["RPERM"],
        })
    res = run_bass_kernel_spmd(nc, in_maps, core_ids=list(range(NCORE)))
    return np.stack([res.results[c]["yT"] for c in range(NCORE)])


def run_l0_attn_a(xT_np, Wq, bq, Wk, bk, Wv, bv, Wp, bp, rtab, upto="all"):
    t = host_tables()
    nc = _get("l0a_" + upto, lambda: build_l0_attn_a(upto))
    in_maps = []
    for c in range(NCORE):
        in_maps.append({
            "xT": np.ascontiguousarray(xT_np[c]),
            "Wq": Wq, "bq": bq, "Wk": Wk, "bk": bk, "Wv": Wv, "bv": bv,
            "Wp": Wp, "bp": bp, "rtab": rtab,
            "hCOSQ": t["RCOSQ"], "hSINQ": t["RSINQ"],
            "hCOS": t["RCOS"], "hSIN": t["RSIN"], "hPERM": t["RPERM"],
            "hOH": t["AOH"], "hMASK": t["AMASK"], "hAID": t["AIDENT"],
        })
    res = run_bass_kernel_spmd(nc, in_maps, core_ids=list(range(NCORE)))
    return np.stack([res.results[c]["yT"] for c in range(NCORE)])


def _maps(base, **kw):
    out = []
    for c in range(NCORE):
        m = dict(kw)
        m["xT"] = np.ascontiguousarray(base[c])
        out.append(m)
    return out


def _maps_percore(base, percore, **kw):
    out = []
    for c in range(NCORE):
        m = dict(kw)
        for k, v in percore.items():
            m[k] = np.ascontiguousarray(v[c])
        m["xT"] = np.ascontiguousarray(base[c])
        out.append(m)
    return out


def kernel(**inputs):
    t = host_tables()
    x = np.asarray(inputs["L0_attn_a_x"], dtype=np.float32)
    xT = np.ascontiguousarray(x.transpose(0, 2, 1))

    def launch(nc, in_maps):
        res = run_bass_kernel_spmd(nc, in_maps, core_ids=list(range(NCORE)))
        return np.stack([np.asarray(res.results[c]["yT"], dtype=np.float32)
                         for c in range(NCORE)])

    ew = np.asarray(inputs["L0_ffn_expert_w"], np.float32).reshape(64 * D, D)
    eb = np.asarray(inputs["L0_ffn_expert_b"], np.float32)
    ew_pc = np.stack([ew[c * 8 * D:(c + 1) * 8 * D] for c in range(NCORE)])
    eb_pc = np.stack([eb[c * 8:(c + 1) * 8] for c in range(NCORE)])
    l0ffn_kw = dict(phi_w=np.asarray(inputs["L0_ffn_phi_w"], np.float32),
                    phi_b=np.asarray(inputs["L0_ffn_phi_b"], np.float32),
                    hID=t["IDENT"], hI32P=t["I32P"], hI32PT=t["I32PT"])
    l1ffn_kw = dict(
        gate_w=np.asarray(inputs["L1_ffn_gate_w"], np.float32),
        gbias=np.asarray(inputs["L1_ffn_bias"], np.float32),
        up_w=np.asarray(inputs["L1_ffn_up_w"], np.float32).reshape(8 * D, 4096),
        up_b=np.asarray(inputs["L1_ffn_up_b"], np.float32),
        down_w=np.asarray(inputs["L1_ffn_down_w"], np.float32).reshape(8 * 4096, D),
        down_b=np.asarray(inputs["L1_ffn_down_b"], np.float32),
        sw_up=np.asarray(inputs["L1_ffn_sw_up"], np.float32),
        sb_up=np.asarray(inputs["L1_ffn_sb_up"], np.float32),
        sw_down=np.asarray(inputs["L1_ffn_sw_down"], np.float32),
        sb_down=np.asarray(inputs["L1_ffn_sb_down"], np.float32),
        hID=t["IDENT"])
    W1qp, b1qp = permute_head_evens_odds(
        np.asarray(inputs["L1_attn_b_Wq"], np.float32),
        np.asarray(inputs["L1_attn_b_bq"], np.float32))
    W1kp, b1kp = permute_head_evens_odds(
        np.asarray(inputs["L1_attn_b_Wk"], np.float32),
        np.asarray(inputs["L1_attn_b_bk"], np.float32))

    xT = launch(_get("l0a", build_l0_attn_a), _maps(
        xT,
        Wq=np.asarray(inputs["L0_attn_a_Wq"], np.float32),
        bq=np.asarray(inputs["L0_attn_a_bq"], np.float32),
        Wk=np.asarray(inputs["L0_attn_a_Wk"], np.float32),
        bk=np.asarray(inputs["L0_attn_a_bk"], np.float32),
        Wv=np.asarray(inputs["L0_attn_a_Wv"], np.float32),
        bv=np.asarray(inputs["L0_attn_a_bv"], np.float32),
        Wp=np.asarray(inputs["L0_attn_a_Wp"], np.float32),
        bp=np.asarray(inputs["L0_attn_a_bp"], np.float32),
        rtab=np.asarray(inputs["L0_attn_a_rel_bias_table"], np.float32),
        hCOSQ=t["RCOSQ"], hSINQ=t["RSINQ"], hCOS=t["RCOS"], hSIN=t["RSIN"],
        hPERM=t["RPERM"], hOH=t["AOH"], hMASK=t["AMASK"], hAID=t["AIDENT"]))
    xT = launch(_get("l0ffn", build_l0_ffn), _maps_percore(
        xT, {"ew_local": ew_pc, "eb_local": eb_pc}, **l0ffn_kw))
    xT = launch(_get("l0b", build_l0_attn_b), _maps(
        xT,
        Wq=np.asarray(inputs["L0_attn_b_Wq"], np.float32),
        Wk=np.asarray(inputs["L0_attn_b_Wk"], np.float32),
        Wv=np.asarray(inputs["L0_attn_b_Wv"], np.float32),
        Wr=permute_cols_evens_odds(np.asarray(inputs["L0_attn_b_Wr"], np.float32)),
        u_bias=np.asarray(inputs["L0_attn_b_u_bias"], np.float32).reshape(-1),
        v_bias=np.asarray(inputs["L0_attn_b_v_bias"], np.float32).reshape(-1),
        hRB=t["RBASIS"], hQC=t["QCOS"], hQS=t["QSIN"], hID=t["IDENT"]))
    xT = launch(_get("l0ffn", build_l0_ffn), _maps_percore(
        xT, {"ew_local": ew_pc, "eb_local": eb_pc}, **l0ffn_kw))
    xT = launch(_get("l1a", build_l1_attn_a), _maps(
        xT, U=np.asarray(inputs["L1_attn_a_U"], np.float32), hID=t["IDENT"]))
    xT = launch(_get("l1ffn", build_l1_ffn), _maps(xT, **l1ffn_kw))
    xT = launch(_get("l1b", build_l1_attn_b), _maps(
        xT, Wq=W1qp, bq=b1qp, Wk=W1kp, bk=b1kp,
        Wv=np.asarray(inputs["L1_attn_b_Wv"], np.float32),
        bv=np.asarray(inputs["L1_attn_b_bv"], np.float32),
        Wp=np.asarray(inputs["L1_attn_b_Wp"], np.float32),
        bp=np.asarray(inputs["L1_attn_b_bp"], np.float32),
        hCOSQ=t["RCOSQ"], hSINQ=t["RSINQ"], hCOS=t["RCOS"], hSIN=t["RSIN"],
        hPERM=t["RPERM"]))
    xT = launch(_get("l1ffn", build_l1_ffn), _maps(xT, **l1ffn_kw))
    return np.ascontiguousarray(xT.transpose(0, 2, 1))
